# version 20
# baseline (speedup 1.0000x reference)
import numpy as np
from contextlib import ExitStack

import concourse.bass as bass
import concourse.mybir as mybir
from concourse.bass_utils import run_bass_kernel_spmd

F32 = mybir.dt.float32
BF16 = mybir.dt.bfloat16
AF = mybir.ActivationFunctionType
ALU = mybir.AluOpType
AX = mybir.AxisListType.X

D = 1024
HQ = 4096
P = 128
DC = D // P
HC = HQ // P
LN_EPS = 1e-5
ABS_EPS = 1e-6
L_LN = float(np.log(10000.0))
L1 = float(np.float32(L_LN))
L2 = float(np.float32(L_LN - L1))
SC21 = 2097152.0


class Ctr:
    def __init__(self, h):
        self.h = h
        self.v = 0

    def inc(self, inst, n=1):
        inst.then_inc(self.h, n)
        self.v += n

    def w(self, eng, v=None):
        eng.wait_ge(self.h, self.v if v is None else v)


def newton_rsqrt(v, r, vneg, t, steps):
    for _ in range(steps):
        v.scalar_tensor_tensor(out=t, in0=r, scalar=vneg, in1=r, op0=ALU.mult, op1=ALU.mult)
        v.drain()
        v.scalar_tensor_tensor(out=r, in0=t, scalar=1.5, in1=r, op0=ALU.add, op1=ALU.mult)
        v.drain()


def rsqrt_col(v, rcol, vcol, tcol, seed_k, seed_c, vneg):
    v.tensor_scalar(out=vneg, in0=vcol, scalar1=-0.5, scalar2=None, op0=ALU.mult)
    v.reciprocal(rcol, vcol)
    v.drain()
    v.tensor_scalar(out=rcol, in0=rcol, scalar1=seed_k, scalar2=seed_c, op0=ALU.mult, op1=ALU.min)
    v.drain()
    newton_rsqrt(v, rcol, vneg, tcol, 6)


def build_program(NT=64, debug=False):
    nc = bass.Bass("TRN2", target_bir_lowering=False, debug=False)
    NG = NT // 2

    kd = "ExternalOutput" if debug else "Internal"
    x0_h = nc.dram_tensor("xseq", [NT, P, D], F32, kind="ExternalInput").ap()
    wq_h = nc.dram_tensor("Wq", [D, D], F32, kind="ExternalInput").ap()
    wk_h = nc.dram_tensor("Wk", [D, D], F32, kind="ExternalInput").ap()
    wv_h = nc.dram_tensor("Wv", [D, D], F32, kind="ExternalInput").ap()
    wfc_h = nc.dram_tensor("Wfc", [HQ, D], F32, kind="ExternalInput").ap()
    lnw_h = nc.dram_tensor("lnw", [HC, P], F32, kind="ExternalInput").ap()
    lnb_h = nc.dram_tensor("lnb", [HC, P], F32, kind="ExternalInput").ap()
    wpj_h = nc.dram_tensor("Wproj", [D, HQ], F32, kind="ExternalInput").ap()
    out_h = nc.dram_tensor("out", [NT // 2, P, D], F32, kind="ExternalOutput").ap()
    pid_h = nc.dram_tensor("pid", [1, 1], mybir.dt.uint32, kind="ExternalInput").ap()
    cmask_h = nc.dram_tensor("cmask", [P, 4 * 256], BF16, kind="ExternalInput").ap()
    cs_h = nc.dram_tensor("cstab", [NT, P, 1024], F32, kind="ExternalInput").ap()

    x1_h = nc.dram_tensor("x1s", [NT, P, D], F32, kind=kd).ap()
    x2_h = nc.dram_tensor("x2s", [NT // 2, P, D], F32, kind=kd).ap()
    m32_hd = nc.dram_tensor("m32s", [1, NT * P], BF16, kind=kd).ap()
    qTl_h = nc.dram_tensor("qTl", [NT // 2, P, D], BF16, kind="Internal").ap()
    x1l_h = nc.dram_tensor("x1l", [NT // 2, P, D], F32, kind="Internal").ap()
    m32l_h = nc.dram_tensor("m32l", [1, (NT // 2) * P], BF16, kind="Internal").ap()
    kT_h = nc.dram_tensor("kTs", [NT, P, D], BF16, kind=kd).ap()
    qT_h = nc.dram_tensor("qTs", [NT, P, D], BF16, kind=kd).ap()
    v_h = nc.dram_tensor("vs", [NT, P, D], BF16, kind=kd).ap()
    wfcT_h = nc.dram_tensor("wfcTs", [P, DC * HQ], BF16, kind=kd).ap()
    wpjT_h = nc.dram_tensor("wpjTs", [P, HC * D], BF16, kind=kd).ap()
    wqT_h = nc.dram_tensor("wqTs", [P, DC * D], BF16, kind=kd).ap()
    wkT_h = nc.dram_tensor("wkTs", [P, DC * D], BF16, kind=kd).ap()
    wvT_h = nc.dram_tensor("wvTs", [P, DC * D], BF16, kind=kd).ap()

    with ExitStack() as top:
        ident_b = top.enter_context(nc.sbuf_tensor("ident_b", [P, P], BF16))
        ident_f = top.enter_context(nc.sbuf_tensor("ident_f", [P, P], F32))
        ones_f = top.enter_context(nc.sbuf_tensor("ones_f", [1, P], F32))
        negones_b = top.enter_context(nc.sbuf_tensor("negones_b", [1, P], BF16))
        onescol_b = top.enter_context(nc.sbuf_tensor("onescol_b", [P, 1], BF16))
        cbh = top.enter_context(nc.sbuf_tensor("cbh", [P, D], F32))
        ct = top.enter_context(nc.sbuf_tensor("ct", [P, 512], F32))
        st = top.enter_context(nc.sbuf_tensor("st", [P, 512], F32))
        rbc = top.enter_context(nc.sbuf_tensor("rbc", [P, 512], F32))
        rbs = top.enter_context(nc.sbuf_tensor("rbs", [P, 512], F32))
        rbc2 = top.enter_context(nc.sbuf_tensor("rbc2", [P, 512], F32))
        rbs2 = top.enter_context(nc.sbuf_tensor("rbs2", [P, 512], F32))
        q2cols = top.enter_context(nc.sbuf_tensor("q2cols", [P, NT], F32))
        k2cols = top.enter_context(nc.sbuf_tensor("k2cols", [P, NT], F32))
        cols = top.enter_context(nc.sbuf_tensor("cols", [P, 16], F32))
        gsum = top.enter_context(nc.sbuf_tensor("gsum", [P, 8], F32))

        sd = Ctr(top.enter_context(nc.semaphore("sd")))
        so = Ctr(top.enter_context(nc.semaphore("so")))
        sp = Ctr(top.enter_context(nc.semaphore("sp")))
        sv = Ctr(top.enter_context(nc.semaphore("sv")))
        ss = Ctr(top.enter_context(nc.semaphore("ss")))
        sg = Ctr(top.enter_context(nc.semaphore("sg")))
        sk = Ctr(top.enter_context(nc.semaphore("sk")))
        sc = Ctr(top.enter_context(nc.semaphore("sc")))
        sq = Ctr(top.enter_context(nc.semaphore("sq")))

        def C(i):
            return cols[:, i:i + 1]

        with ExitStack() as ph:
            pp_a = ph.enter_context(nc.psum_tensor("pp_a", [P, 512], F32))
            pp_b = ph.enter_context(nc.psum_tensor("pp_b", [P, 512], F32))
            pp_t = ph.enter_context(nc.psum_tensor("pp_t", [P, 1024], BF16))
            pp_c = ph.enter_context(nc.psum_tensor("pp_c", [P, 512], F32))
            pp_d = ph.enter_context(nc.psum_tensor("pp_d", [P, 512], F32))
            wstr = ph.enter_context(nc.sbuf_tensor("wstr", [P, HQ], F32))
            wstr2 = ph.enter_context(nc.sbuf_tensor("wstr2", [P, HQ], F32))
            g1 = ph.enter_context(nc.sbuf_tensor("g1", [P, HQ], BF16))
            g2 = ph.enter_context(nc.sbuf_tensor("g2", [P, HQ], BF16))
            wqn = ph.enter_context(nc.sbuf_tensor("wqn", [P, HQ], BF16))
            wimg = ph.enter_context(nc.sbuf_tensor("wimg", [P, DC * HQ], BF16))
            acol = ph.enter_context(nc.sbuf_tensor("acol", [P, 34], F32))
            wcols = ph.enter_context(nc.sbuf_tensor("wcols", [P, HC], F32))
            bcols = ph.enter_context(nc.sbuf_tensor("bcols", [P, HC], BF16))
            wrow = ph.enter_context(nc.sbuf_tensor("wrow", [HC, P], F32))
            rowb = ph.enter_context(nc.sbuf_tensor("rowb", [1, 2048], F32))
            jrow = ph.enter_context(nc.sbuf_tensor("jrow", [1, 512], F32))
            e2r = ph.enter_context(nc.sbuf_tensor("e2r", [1, 512], F32))
            rot = ph.enter_context(nc.sbuf_tensor("rot", [1, 4 * 512], F32))
            rtm = ph.enter_context(nc.sbuf_tensor("rtm", [1, 3 * 512], F32))
            tA = ph.enter_context(nc.sbuf_tensor("tA", [64, 512], F32))
            tB = ph.enter_context(nc.sbuf_tensor("tB", [64, 512], F32))
            tC = ph.enter_context(nc.sbuf_tensor("tC", [64, 512], F32))
            tD = ph.enter_context(nc.sbuf_tensor("tD", [64, 512], F32))
            c2 = ph.enter_context(nc.sbuf_tensor("c2", [64, 512], F32))
            s2 = ph.enter_context(nc.sbuf_tensor("s2", [64, 512], F32))

            def rc(k):
                return rot[0:1, (k % 2) * 1024:(k % 2) * 1024 + 512]

            def rs_(k):
                return rot[0:1, (k % 2) * 1024 + 512:(k % 2) * 1024 + 1024]

            blk = ph.enter_context(nc.Block())

            @blk.gpsimd
            def _(gp):
                gp.memset(ident_b[:, :], 1.0)
                gp.memset(ident_f[:, :], 1.0)
                gp.drain()
                sg.inc(gp.affine_select(out=ident_b[:, :], in_=ident_b[:, :],
                                        compare_op=ALU.is_equal, fill=0.0, base=0,
                                        pattern=[[-1, P]], channel_multiplier=1))
                sg.inc(gp.affine_select(out=ident_f[:, :], in_=ident_f[:, :],
                                        compare_op=ALU.is_equal, fill=0.0, base=0,
                                        pattern=[[-1, P]], channel_multiplier=1))
                sg.inc(gp.memset(ones_f[0:1, :], 1.0))
                sg.inc(gp.memset(negones_b[0:1, :], -1.0))
                sg.inc(gp.memset(onescol_b[:, 0:1], 1.0))
                sg.inc(gp.iota(jrow[0:1, :], [[1, 512]], base=0, channel_multiplier=0,
                               allow_small_or_imprecise_dtypes=True))
                sg.inc(gp.memset(ct[0:1, :], 1.0))
                sg.inc(gp.memset(st[0:1, :], 0.0))

            @blk.sync
            def _(sy):
                sd.w(sy)
                sd.inc(sy.dma_start(out=wrow[0:HC, :], in_=lnw_h[:, :]), 16)
                sd.w(sy)
                sk.inc(sy.nop())

            @blk.tensor
            def _(te):
                sk.w(te)
                sg.w(te)
                ss.w(te)
                sp.inc(te.transpose(pp_a[:, 0:HC], wrow[0:HC, :], ident_f[0:HC, 0:HC]))

            @blk.vector
            def _(v):
                sp.w(v)
                a = v.tensor_scalar(out=wcols[:, :], in0=pp_a[:, 0:HC], scalar1=0.5,
                                    scalar2=None, op0=ALU.mult)
                v.drain()
                sv.inc(a)

            @blk.sync
            def _(sy):
                sv.w(sy)
                sd.inc(sy.dma_start(out=wrow[0:HC, :], in_=lnb_h[:, :]), 16)
                sd.w(sy)
                sk.inc(sy.nop())

            @blk.tensor
            def _(te):
                sk.w(te)
                sp.inc(te.transpose(pp_a[:, 0:HC], wrow[0:HC, :], ident_f[0:HC, 0:HC]))

            @blk.scalar
            def _(s):
                sp.w(s)
                ss.inc(s.copy(bcols[:, :], pp_a[:, 0:HC]))

            for wi, (src, n_tile, tile_w, img_dst, img_w) in enumerate(
                    [(wfc_h, HC, D, wfcT_h, DC * HQ), (wpj_h, DC, HQ, wpjT_h, HC * D)]):
                src_t = src.rearrange("(n p) w -> n p w", p=P)
                for r in range(n_tile):
                    @blk.sync
                    def _(sy, r=r, src_t=src_t, tile_w=tile_w):
                        if r >= 2:
                            sv.w(sy)
                        buf = wstr if r % 2 == 0 else wstr2
                        sd.inc(sy.dma_start(out=buf[:, 0:tile_w], in_=src_t[r]), 16)
                        sd.w(sy)
                        sk.inc(sy.nop())

                    @blk.vector
                    def _(v, r=r, tile_w=tile_w):
                        sk.w(v)
                        buf = wstr if r % 2 == 0 else wstr2
                        a = v.reduce_sum(acol[:, r:r + 1], buf[:, 0:tile_w], axis=AX,
                                         apply_absolute_value=True)
                        v.drain()
                        sv.inc(a)

                @blk.vector
                def _(v, n_tile=n_tile):
                    a = v.reduce_sum(C(0), acol[:, 0:n_tile], axis=AX)
                    v.drain()
                    sv.inc(a)

                @blk.tensor
                def _(te):
                    sv.w(te)
                    sp.inc(te.transpose(pp_c[0:1, 0:P], C(0), ident_f[:, :]))

                @blk.vector
                def _(v):
                    sp.w(v)
                    v.reduce_sum(rowb[0:1, 0:1], pp_c[0:1, 0:P], axis=AX)
                    v.drain()
                    v.tensor_scalar(out=rowb[0:1, 0:1], in0=rowb[0:1, 0:1],
                                    scalar1=1.0 / (HQ * D), scalar2=ABS_EPS,
                                    op0=ALU.mult, op1=ALU.add)
                    v.drain()
                    a = v.tensor_scalar(out=rowb[0:1, 1:2], in0=rowb[0:1, 0:1],
                                        scalar1=0.5, scalar2=None, op0=ALU.mult)
                    v.drain()
                    sv.inc(a)

                @blk.tensor
                def _(te):
                    sv.w(te)
                    sp.inc(te.matmul(pp_c[:, 0:1], ones_f[0:1, :], rowb[0:1, 1:2],
                                     start=True, stop=True))

                @blk.vector
                def _(v):
                    sp.w(v)
                    v.tensor_copy(C(1), pp_c[:, 0:1])
                    v.drain()
                    a = v.tensor_scalar(out=C(2), in0=C(1), scalar1=-1.0, scalar2=None, op0=ALU.mult)
                    v.drain()
                    sv.inc(a)

                nfc = tile_w // P
                for r in range(n_tile):
                    @blk.sync
                    def _(sy, r=r, src_t=src_t, tile_w=tile_w):
                        sv.w(sy)
                        buf = wstr if r % 2 == 0 else wstr2
                        sd.inc(sy.dma_start(out=buf[:, 0:tile_w], in_=src_t[r]), 16)
                        sd.w(sy)
                        sk.inc(sy.nop())

                    @blk.vector
                    def _(v, r=r, tile_w=tile_w):
                        sk.w(v)
                        sp.w(v)
                        buf = wstr if r % 2 == 0 else wstr2
                        v.tensor_scalar(out=g1[:, 0:tile_w], in0=buf[:, 0:tile_w], scalar1=C(1),
                                        scalar2=None, op0=ALU.is_gt)
                        v.tensor_scalar(out=g2[:, 0:tile_w], in0=buf[:, 0:tile_w], scalar1=C(2),
                                        scalar2=None, op0=ALU.is_lt)
                        v.drain()
                        a = v.tensor_sub(wqn[:, 0:tile_w], g1[:, 0:tile_w], g2[:, 0:tile_w])
                        v.drain()
                        sv.inc(a)

                    for cg0 in range(0, nfc, 8):
                        @blk.tensor
                        def _(te, r=r, cg0=cg0):
                            sv.w(te)
                            ss.w(te)
                            for k in range(min(8, nfc - cg0)):
                                sp.inc(te.transpose(pp_t[:, k * P:(k + 1) * P],
                                                    wqn[:, (cg0 + k) * P:(cg0 + k + 1) * P],
                                                    ident_b[:, :]))

                        @blk.scalar
                        def _(s, r=r, cg0=cg0, wi=wi):
                            sp.w(s)
                            for k in range(min(8, nfc - cg0)):
                                cg = cg0 + k
                                if wi == 0:
                                    dst = wimg[:, cg * HQ + r * P: cg * HQ + r * P + P]
                                else:
                                    dst = wimg[:, cg * D + r * P: cg * D + r * P + P]
                                ss.inc(s.copy(dst, pp_t[:, k * P:(k + 1) * P]))

                if wi == 1:
                    @blk.tensor
                    def _(te):
                        ss.w(te)
                        for hc in range(HC):
                            a = te.matmul(pp_c[0:1, 0:512], bcols[:, hc:hc + 1],
                                          wimg[:, hc * D:hc * D + 512],
                                          start=(hc == 0), stop=(hc == HC - 1))
                            a2 = te.matmul(pp_d[0:1, 0:512], bcols[:, hc:hc + 1],
                                           wimg[:, hc * D + 512:hc * D + 1024],
                                           start=(hc == 0), stop=(hc == HC - 1))
                        sp.inc(a)
                        sp.inc(a2)

                    @blk.vector
                    def _(v):
                        sp.w(v)
                        v.tensor_scalar(out=rowb[0:1, 0:512], in0=pp_c[0:1, 0:512],
                                        scalar1=0.5, scalar2=None, op0=ALU.mult)
                        a = v.tensor_scalar(out=rowb[0:1, 512:1024], in0=pp_d[0:1, 0:512],
                                            scalar1=0.5, scalar2=None, op0=ALU.mult)
                        v.drain()
                        sv.inc(a)
                        for hc in range(HC):
                            a = v.tensor_scalar(out=wimg[:, hc * D:(hc + 1) * D],
                                                in0=wimg[:, hc * D:(hc + 1) * D],
                                                scalar1=wcols[:, hc:hc + 1], scalar2=None,
                                                op0=ALU.mult)
                            sv.inc(a)
                        v.drain()

                    @blk.tensor
                    def _(te):
                        sv.w(te)
                        sp.inc(te.matmul(pp_a[:, :], ones_f[0:1, :], rowb[0:1, 0:512],
                                         start=True, stop=True))
                        sp.inc(te.matmul(pp_b[:, :], ones_f[0:1, :], rowb[0:1, 512:1024],
                                         start=True, stop=True))

                    @blk.scalar
                    def _(s):
                        sp.w(s)
                        ss.inc(s.copy(cbh[:, 0:512], pp_a[:, :]))
                        ss.inc(s.copy(cbh[:, 512:1024], pp_b[:, :]))

                @blk.sync
                def _(sy, img_dst=img_dst, img_w=img_w):
                    ss.w(sy)
                    sv.w(sy)
                    so.inc(sy.dma_start(out=img_dst[:, 0:img_w], in_=wimg[:, 0:img_w]), 16)
                    so.w(sy)

            for src, img_dst in [(wq_h, wqT_h), (wk_h, wkT_h), (wv_h, wvT_h)]:
                src_t = src.rearrange("(n p) w -> n p w", p=P)
                for r in range(DC):
                    @blk.sync
                    def _(sy, r=r, src_t=src_t):
                        if r >= 2:
                            sv.w(sy)
                        buf = wstr if r % 2 == 0 else wstr2
                        sd.inc(sy.dma_start(out=buf[:, 0:D], in_=src_t[r]), 16)
                        sd.w(sy)
                        sk.inc(sy.nop())

                    @blk.vector
                    def _(v, r=r):
                        sk.w(v)
                        sp.w(v)
                        buf = wstr if r % 2 == 0 else wstr2
                        a = v.tensor_copy(wqn[:, 0:D], buf[:, 0:D])
                        v.drain()
                        sv.inc(a)

                    @blk.tensor
                    def _(te, r=r):
                        sv.w(te)
                        ss.w(te)
                        for cg in range(DC):
                            sp.inc(te.transpose(pp_t[:, cg * P:(cg + 1) * P],
                                                wqn[:, cg * P:(cg + 1) * P], ident_b[:, :]))

                    @blk.scalar
                    def _(s, r=r):
                        sp.w(s)
                        for cg in range(DC):
                            ss.inc(s.copy(wimg[:, cg * D + r * P: cg * D + r * P + P],
                                          pp_t[:, cg * P:(cg + 1) * P]))

                @blk.sync
                def _(sy, img_dst=img_dst):
                    ss.w(sy)
                    so.inc(sy.dma_start(out=img_dst[:, 0:DC * D], in_=wimg[:, 0:DC * D]), 16)
                    so.w(sy)

        def ffn_phase(src_ap, dst_ap, n_tiles, name):
            seed_kc = (1.0, 1.0) if name == "a1" else (32.0, 0.05)
            with ExitStack() as ph:
                wfcT = ph.enter_context(nc.sbuf_tensor(f"{name}_wfcT", [P, DC * HQ], BF16))
                wpjT = ph.enter_context(nc.sbuf_tensor(f"{name}_wpjT", [P, HC * D], BF16))
                xb = ph.enter_context(nc.sbuf_tensor(f"{name}_xb", [P, 3 * D], F32))
                xcb = ph.enter_context(nc.sbuf_tensor(f"{name}_xcb", [P, 2 * D], BF16))
                xcT = ph.enter_context(nc.sbuf_tensor(f"{name}_xcT", [P, 2 * D], BF16))
                g = ph.enter_context(nc.sbuf_tensor(f"{name}_g", [P, HQ], BF16))
                gb = ph.enter_context(nc.sbuf_tensor(f"{name}_gb", [P, HQ], BF16))
                gbT = ph.enter_context(nc.sbuf_tensor(f"{name}_gbT", [P, HQ], BF16))
                dump = ph.enter_context(nc.sbuf_tensor(f"{name}_dump", [P, HQ], BF16))
                x1b = ph.enter_context(nc.sbuf_tensor(f"{name}_x1b", [P, 2 * D], F32))
                psT = [ph.enter_context(nc.psum_tensor(f"{name}_psT{i}", [P, 1024], BF16)) for i in range(2)]
                psfc = [ph.enter_context(nc.psum_tensor(f"{name}_psfc{i}", [P, 512], F32)) for i in range(4)]
                pp = [ph.enter_context(nc.psum_tensor(f"{name}_pp{i}", [P, 512], F32)) for i in range(2)]
                blk = ph.enter_context(nc.Block())

                @blk.sync
                def _(sy):
                    sd.w(sy)
                    sd.inc(sy.dma_start(out=wfcT[:, :], in_=wfcT_h[:, :]), 16)
                    sd.inc(sy.dma_start(out=wpjT[:, :], in_=wpjT_h[:, :]), 16)
                    sd.w(sy)
                    sk.inc(sy.nop())

                wt_ready = sk.v
                ss_base0 = ss.v
                load_mark = {}
                transA_mark = {}
                xcb_mark = {}
                rstd1_mark = {}
                evacT_mark = {}
                fc_marks = {}
                gelu_mark = {}
                gb_mark = {}
                rstd2_mark = {}
                transB_marks = {}
                transB_done = {}
                proj_mark = {}
                x1_done = {-1: sv.v}
                pool = {"i": 0, "hist": []}

                def pool_take():
                    slot = pool["i"] % 2
                    pool["i"] += 1
                    for s2, mk in reversed(pool["hist"]):
                        if s2 == slot:
                            return slot, mk
                    return slot, (ss, 0)

                def emit_load(t):
                    if t >= n_tiles:
                        return

                    @blk.sync
                    def _(sy, t=t):
                        if t >= 3:
                            sv.w(sy, x1_done[t - 3])
                            so.w(sy)
                        sd.inc(sy.dma_start(out=xb[:, (t % 3) * D:(t % 3 + 1) * D], in_=src_ap[t]), 16)
                        sd.w(sy)
                        sk.inc(sy.nop())
                    load_mark[t] = sk.v

                def emit_stats(t):
                    if t >= n_tiles:
                        return

                    @blk.vector
                    def _(v, t=t):
                        v.wait_ge(sk.h, load_mark[t])
                        if t >= 2:
                            v.wait_ge(sp.h, transA_mark[t - 2])
                            v.wait_ge(ss.h, gelu_mark[t - 2])
                        x = xb[:, (t % 3) * D:(t % 3 + 1) * D]
                        xc = xcb[:, (t % 2) * D:(t % 2 + 1) * D]
                        v.reduce_sum(C(0), x, axis=AX)
                        v.drain()
                        v.tensor_scalar(out=C(0), in0=C(0), scalar1=1.0 / D, scalar2=None, op0=ALU.mult)
                        v.drain()
                        a = v.tensor_scalar(out=xc, in0=x, scalar1=C(0), scalar2=None, op0=ALU.subtract)
                        v.drain()
                        sv.inc(a)
                        v.scalar_tensor_tensor(out=dump[:, 0:D], in0=xc, scalar=1.0, in1=xc,
                                               op0=ALU.mult, op1=ALU.mult, accum_out=C(1))
                        v.drain()
                        v.tensor_scalar(out=C(2), in0=C(1), scalar1=1.0 / D, scalar2=LN_EPS,
                                        op0=ALU.mult, op1=ALU.add)
                        v.drain()
                        rsqrt_col(v, C(3), C(2), C(6), *seed_kc, C(11))
                        a = v.tensor_copy(cols[:, 7 + (t % 2):8 + (t % 2)], C(3))
                        v.drain()
                        sv.inc(a)
                    xcb_mark[t] = sv.v - 1
                    rstd1_mark[t] = sv.v

                def emit_gb(t):
                    @blk.vector
                    def _(v, t=t):
                        v.wait_ge(ss.h, gelu_mark[t])
                        if t >= 1:
                            v.wait_ge(sp.h, transB_done[t - 1])
                        v.reduce_sum(C(4), gsum[:, 0:8], axis=AX)
                        v.drain()
                        v.tensor_scalar(out=C(4), in0=C(4), scalar1=1.0 / HQ, scalar2=None, op0=ALU.mult)
                        v.drain()
                        a = v.tensor_scalar(out=gb[:, :], in0=g[:, :], scalar1=C(4), scalar2=None,
                                            op0=ALU.subtract)
                        v.drain()
                        sv.inc(a)
                        v.scalar_tensor_tensor(out=dump[:, :], in0=gb[:, :], scalar=1.0, in1=gb[:, :],
                                               op0=ALU.mult, op1=ALU.mult, accum_out=C(1))
                        v.drain()
                        v.tensor_scalar(out=C(2), in0=C(1), scalar1=1.0 / HQ, scalar2=LN_EPS,
                                        op0=ALU.mult, op1=ALU.add)
                        v.drain()
                        rsqrt_col(v, C(5), C(2), C(6), 16.0, 0.08, C(11))
                        a = v.tensor_copy(cols[:, 9 + (t % 2):10 + (t % 2)], C(5))
                        v.drain()
                        sv.inc(a)
                    gb_mark[t] = sv.v - 1
                    rstd2_mark[t] = sv.v

                def emit_transA(t):
                    if t >= n_tiles:
                        return
                    slot, (gsem, gmk) = pool_take()

                    @blk.tensor
                    def _(te, t=t, slot=slot, gsem=gsem, gmk=gmk):
                        sv.w(te, xcb_mark[t])
                        te.wait_ge(gsem.h, gmk)
                        for cg in range(DC):
                            sp.inc(te.transpose(psT[slot][:, cg * P:(cg + 1) * P],
                                                xcb[:, (t % 2) * D + cg * P:(t % 2) * D + (cg + 1) * P],
                                                ident_b[:, :]))
                    transA_mark[t] = sp.v
                    pool["hist"].append((slot, (ss, None)))
                    return slot

                def emit_transB(t):
                    slots = []
                    for grp in range(4):
                        slot, (gsem, gmk) = pool_take()

                        @blk.tensor
                        def _(te, t=t, grp=grp, slot=slot, gsem=gsem, gmk=gmk):
                            sv.w(te, gb_mark[t])
                            te.wait_ge(gsem.h, gmk)
                            for k in range(8):
                                hc = grp * 8 + k
                                sp.inc(te.transpose(psT[slot][:, k * P:(k + 1) * P],
                                                    gb[:, hc * P:(hc + 1) * P], ident_b[:, :]))
                        transB_marks.setdefault(t, []).append(sp.v)
                        pool["hist"].append((slot, (ss, None)))
                        slots.append(slot)
                    transB_done[t] = sp.v
                    return slots

                def emit_fc(t):
                    if t >= n_tiles:
                        return
                    ms = []
                    for q in range(4):
                        @blk.tensor
                        def _(te, t=t, q=q):
                            te.wait_ge(sv.h, evacT_mark[t])
                            gm = ss_base0 + 8 * t
                            if q < 2:
                                if t >= 1:
                                    te.wait_ge(ss.h, gm - 2 + 2 * q)
                            else:
                                te.wait_ge(ss.h, gm + 2 * (q - 1))
                            te.wait_ge(sk.h, wt_ready)
                            for dc in range(DC):
                                for hg2 in range(2):
                                    off = dc * HQ + q * 1024 + hg2 * 512
                                    a = te.matmul(psfc[2 * (q % 2) + hg2][:, :],
                                                  xcT[:, (t % 2) * D + dc * P:(t % 2) * D + (dc + 1) * P],
                                                  wfcT[:, off:off + 512],
                                                  start=(dc == 0), stop=(dc == DC - 1))
                                    if dc == DC - 1:
                                        sp.inc(a)
                        ms.append(sp.v)
                    fc_marks[t] = ms

                def emit_proj(t):
                    @blk.tensor
                    def _(te, t=t):
                        sv2, m2 = bevac_mark[t]
                        te.wait_ge(sv2.h, m2)
                        te.wait_ge(sv.h, x1_done[t - 1])
                        for hc in range(HC):
                            a = te.matmul(pp[0][:, :], gbT[:, hc * P:(hc + 1) * P],
                                          wpjT[:, hc * D:hc * D + 512],
                                          start=(hc == 0), stop=(hc == HC - 1))
                            a2 = te.matmul(pp[1][:, :], gbT[:, hc * P:(hc + 1) * P],
                                           wpjT[:, hc * D + 512:hc * D + 1024],
                                           start=(hc == 0), stop=(hc == HC - 1))
                        sp.inc(a)
                        sp.inc(a2)
                    proj_mark[t] = sp.v

                bevac_mark = {}
                tB_map = {}

                def emit_evacT(t):
                    if t >= n_tiles:
                        return

                    @blk.vector
                    def _(v, t=t):
                        v.wait_ge(sp.h, transA_mark[t])
                        a = v.tensor_copy(xcT[:, (t % 2) * D:(t % 2 + 1) * D],
                                          psT[tA_map[t]][:, :])
                        v.drain()
                        sv.inc(a)
                    evacT_mark[t] = sv.v
                    for i2 in range(len(pool["hist"]) - 1, -1, -1):
                        if pool["hist"][i2][1][1] is None and pool["hist"][i2][0] == tA_map[t]:
                            pool["hist"][i2] = (tA_map[t], (sv, sv.v))
                            break

                def emit_transB_half(t, half):
                    slots = []
                    for grp in (0, 1) if half == 0 else (2, 3):
                        slot, (gsem, gmk) = pool_take()

                        @blk.tensor
                        def _(te, t=t, grp=grp, slot=slot, gsem=gsem, gmk=gmk):
                            sv.w(te, gb_mark[t])
                            te.wait_ge(gsem.h, gmk)
                            for k in range(8):
                                hc = grp * 8 + k
                                sp.inc(te.transpose(psT[slot][:, k * P:(k + 1) * P],
                                                    gb[:, hc * P:(hc + 1) * P], ident_b[:, :]))
                        transB_marks.setdefault(t, []).append(sp.v)
                        pool["hist"].append((slot, (sv, None)))
                        slots.append(slot)
                    return slots

                def emit_bevacs_half(t, half, slots):
                    @blk.vector
                    def _(v, t=t, half=half, slots=slots):
                        if half == 0 and t >= 1:
                            v.wait_ge(sp.h, proj_mark[t - 1])
                        for i3, grp in enumerate((0, 1) if half == 0 else (2, 3)):
                            v.wait_ge(sp.h, transB_marks[t][grp])
                            a = v.tensor_copy(gbT[:, grp * 1024:(grp + 1) * 1024],
                                              psT[slots[i3]][:, :])
                            v.drain()
                            sv.inc(a)
                    for i3, grp in enumerate((0, 1) if half == 0 else (2, 3)):
                        mk = sv.v - (1 - i3)
                        for i2 in range(len(pool["hist"])):
                            if pool["hist"][i2][1][1] is None and pool["hist"][i2][0] == slots[i3]:
                                pool["hist"][i2] = (slots[i3], (sv, mk))
                                break
                    if half == 1:
                        bevac_mark[t] = (sv, sv.v)

                def emit_gelu(t):
                    if t >= n_tiles:
                        return

                    @blk.scalar
                    def _(s, t=t):
                        s.wait_ge(sv.h, rstd1_mark[t])
                        if t >= 1:
                            s.wait_ge(sv.h, gb_mark[t - 1] + 1)
                        rs = cols[:, 7 + (t % 2):8 + (t % 2)]
                        for q in range(4):
                            s.wait_ge(sp.h, fc_marks[t][q])
                            for hg2 in range(2):
                                off = q * 1024 + hg2 * 512
                                ss.inc(s.activation(out=g[:, off:off + 512],
                                                    in_=psfc[2 * (q % 2) + hg2][:, :],
                                                    func=AF.Gelu, bias=0.0, scale=rs,
                                                    accum_out=gsum[:, q * 2 + hg2:q * 2 + hg2 + 1]))
                    gelu_mark[t] = ss.v

                x1_store_mark = {}

                def emit_x1(t):
                    if t < 0:
                        return

                    @blk.vector
                    def _(v, t=t):
                        v.wait_ge(sp.h, proj_mark[t])
                        v.wait_ge(sv.h, rstd2_mark[t])
                        if t >= 2:
                            v.wait_ge(sk.h, x1_store_mark[t - 2])
                        x = xb[:, (t % 3) * D:(t % 3 + 1) * D]
                        rs = cols[:, 9 + (t % 2):10 + (t % 2)]
                        for h2 in range(2):
                            v.scalar_tensor_tensor(
                                out=x1b[:, (t % 2) * D + h2 * 512:(t % 2) * D + (h2 + 1) * 512],
                                in0=pp[h2][:, :], scalar=rs,
                                in1=cbh[:, h2 * 512:(h2 + 1) * 512],
                                op0=ALU.mult, op1=ALU.add)
                        v.drain()
                        a = v.tensor_add(x1b[:, (t % 2) * D:(t % 2 + 1) * D],
                                         x1b[:, (t % 2) * D:(t % 2 + 1) * D], x)
                        v.drain()
                        sv.inc(a)
                    x1_done[t] = sv.v

                    @blk.sync
                    def _(sy, t=t):
                        sv.w(sy)
                        so.inc(sy.dma_start(out=dst_ap[t],
                                            in_=x1b[:, (t % 2) * D:(t % 2 + 1) * D]), 16)
                        so.w(sy)
                        sk.inc(sy.nop())
                    x1_store_mark[t] = sk.v

                tA_map = {}

                def do_transA(t):
                    if t >= n_tiles:
                        return
                    tA_map[t] = emit_transA(t)

                emit_load(0)
                emit_stats(0)
                do_transA(0)
                emit_evacT(0)
                emit_load(1)
                emit_stats(1)
                emit_fc(0)
                emit_gelu(0)

                for t in range(n_tiles):
                    do_transA(t + 1)
                    emit_evacT(t + 1)
                    emit_fc(t + 1)
                    emit_gb(t)
                    emit_gelu(t + 1)
                    s01 = emit_transB_half(t, 0)
                    if t >= 1:
                        emit_proj(t - 1)
                        emit_x1(t - 1)
                    emit_bevacs_half(t, 0, s01)
                    s23 = emit_transB_half(t, 1)
                    emit_bevacs_half(t, 1, s23)
                    transB_done[t] = transB_marks[t][3]
                    emit_load(t + 2)
                    emit_stats(t + 2)

                emit_proj(n_tiles - 1)
                emit_x1(n_tiles - 1)

                @blk.sync
                def _(sy):
                    so.w(sy)

        def qkv_phase(m32bf):
            with ExitStack() as ph:
                wqT = ph.enter_context(nc.sbuf_tensor("a2_wqT", [P, DC * D], BF16))
                wkT = ph.enter_context(nc.sbuf_tensor("a2_wkT", [P, DC * D], BF16))
                wvT = ph.enter_context(nc.sbuf_tensor("a2_wvT", [P, DC * D], BF16))
                xb = ph.enter_context(nc.sbuf_tensor("a2_xb", [P, 3 * D], F32))
                xnb = ph.enter_context(nc.sbuf_tensor("a2_xnb", [P, 2 * D], BF16))
                xnT = ph.enter_context(nc.sbuf_tensor("a2_xnT", [P, 2 * D], BF16))
                qb = ph.enter_context(nc.sbuf_tensor("a2_qb", [P, 2 * D], BF16))
                kb = ph.enter_context(nc.sbuf_tensor("a2_kb", [P, 2 * D], BF16))
                vbv = ph.enter_context(nc.sbuf_tensor("a2_vb", [P, 2 * D], BF16))
                qp = ph.enter_context(nc.sbuf_tensor("a2_qp", [P, 2 * D], BF16))
                kp = ph.enter_context(nc.sbuf_tensor("a2_kp", [P, 2 * D], BF16))
                qpT = ph.enter_context(nc.sbuf_tensor("a2_qpT", [P, 2 * D], BF16))
                kpT = ph.enter_context(nc.sbuf_tensor("a2_kpT", [P, 2 * D], BF16))
                rtq = ph.enter_context(nc.sbuf_tensor("a2_rtq", [P, 1024], F32))
                rtk = ph.enter_context(nc.sbuf_tensor("a2_rtk", [P, 1024], F32))
                dmp = ph.enter_context(nc.sbuf_tensor("a2_dmp", [P, D], BF16))
                cse = ph.enter_context(nc.sbuf_tensor("a2_cse", [P, 1024], F32))
                cso = ph.enter_context(nc.sbuf_tensor("a2_cso", [P, 1024], F32))
                tmr = [ph.enter_context(nc.sbuf_tensor(f"a2_tm{i}", [P, 512], F32)) for i in range(4)]
                gtm = [ph.enter_context(nc.sbuf_tensor(f"a2_gtm{i}", [P, 512], F32)) for i in range(4)]
                row32 = ph.enter_context(nc.sbuf_tensor("row32", [1, NT * P], F32))
                psT = ph.enter_context(nc.psum_tensor("a2_psT", [P, 1024], BF16))
                psTqk = ph.enter_context(nc.psum_tensor("a2_psTqk", [P, 1024], BF16))
                psQ = [ph.enter_context(nc.psum_tensor(f"a2_psQ{i}", [P, 512], F32)) for i in range(2)]
                psK = [ph.enter_context(nc.psum_tensor(f"a2_psK{i}", [P, 512], F32)) for i in range(2)]
                psV = [ph.enter_context(nc.psum_tensor(f"a2_psV{i}", [P, 512], F32)) for i in range(2)]
                blk = ph.enter_context(nc.Block())

                @blk.sync
                def _(sy):
                    sd.w(sy)
                    sd.inc(sy.dma_start(out=wqT[:, :], in_=wqT_h[:, 0:DC * D]), 16)
                    sd.inc(sy.dma_start(out=wkT[:, :], in_=wkT_h[:, 0:DC * D]), 16)
                    sd.inc(sy.dma_start(out=wvT[:, :], in_=wvT_h[:, 0:DC * D]), 16)
                    sd.w(sy)
                    sk.inc(sy.nop())

                wt_ready = sk.v
                load_mark = {}
                xnb_mark = {}
                transA_mark = {}
                xnT_mark = {}
                qkv_mark = {}
                qkbv_mark = {}
                qpT_mark = {}
                kpT_mark = {}
                ropeq_mark = {}
                ropek_mark = {}
                q2_mark = {}
                k2_mark = {}
                cs_mark = {}
                qT_trans_mark = {}
                kT_trans_mark = {}
                store_mark = {}

                @blk.gpsimd
                def _(gp):
                    sc.inc(gp.dma_start(out=cse[:, :], in_=cs_h[0]), 16)
                    sc.inc(gp.dma_start(out=cso[:, :], in_=cs_h[1]), 16)
                    sc.w(gp)
                    sq.inc(gp.nop())
                cs_mark[0] = (sq, sq.v)
                cs_mark[1] = (sq, sq.v)

                def cs_of(t):
                    return cse if t % 2 == 0 else cso

                def emit_load(t):
                    if t >= NT:
                        return

                    @blk.sync
                    def _(sy, t=t):
                        if t >= 3:
                            sv.w(sy, q2_mark[t - 3])
                            so.w(sy)
                        sd.inc(sy.dma_start(out=xb[:, (t % 3) * D:(t % 3 + 1) * D], in_=x1_h[t]), 16)
                        sd.w(sy)
                        sk.inc(sy.nop())
                    load_mark[t] = sk.v

                def emit_stats(t):
                    if t >= NT:
                        return

                    @blk.vector
                    def _(v, t=t):
                        v.wait_ge(sk.h, load_mark[t])
                        if t >= 2:
                            v.wait_ge(sp.h, transA_mark[t - 2])
                        x = xb[:, (t % 3) * D:(t % 3 + 1) * D]
                        v.reduce_sum(C(0), x, axis=AX)
                        v.scalar_tensor_tensor(out=dmp[:, 0:D], in0=x, scalar=1.0, in1=x,
                                               op0=ALU.mult, op1=ALU.mult, accum_out=C(1))
                        v.drain()
                        v.tensor_scalar(out=C(0), in0=C(0), scalar1=1.0 / D, scalar2=None, op0=ALU.mult)
                        v.drain()
                        v.tensor_mul(C(6), C(0), C(0))
                        v.tensor_scalar(out=C(2), in0=C(1), scalar1=1.0 / D, scalar2=None, op0=ALU.mult)
                        v.drain()
                        v.tensor_sub(C(2), C(2), C(6))
                        v.drain()
                        v.tensor_scalar(out=C(2), in0=C(2), scalar1=1.0, scalar2=LN_EPS,
                                        op0=ALU.mult, op1=ALU.add)
                        v.drain()
                        rsqrt_col(v, C(3), C(2), C(6), 32.0, 0.05, C(11))
                        a = v.tensor_scalar(out=xnb[:, (t % 2) * D:(t % 2 + 1) * D], in0=x,
                                            scalar1=C(0), scalar2=C(3),
                                            op0=ALU.subtract, op1=ALU.mult)
                        v.drain()
                        sv.inc(a)
                    xnb_mark[t] = sv.v

                def emit_transA(t):
                    if t >= NT:
                        return

                    @blk.tensor
                    def _(te, t=t):
                        sv.w(te, xnb_mark[t])
                        if t >= 1:
                            te.wait_ge(ss.h, xnT_mark[t - 1])
                        for cg in range(DC):
                            sp.inc(te.transpose(psT[:, cg * P:(cg + 1) * P],
                                                xnb[:, (t % 2) * D + cg * P:(t % 2) * D + (cg + 1) * P],
                                                ident_b[:, :]))
                    transA_mark[t] = sp.v

                def emit_qkv(t):
                    if t >= NT:
                        return

                    @blk.tensor
                    def _(te, t=t):
                        te.wait_ge(ss.h, xnT_mark[t])
                        te.wait_ge(sk.h, wt_ready)
                        if t >= 1:
                            te.wait_ge(sv.h, ropeq_mark[t - 1])
                            te.wait_ge(sg.h, ropek_mark[t - 1])
                            te.wait_ge(ss.h, qkbv_mark[t - 1])
                        for dc in range(DC):
                            lhs = xnT[:, (t % 2) * D + dc * P:(t % 2) * D + (dc + 1) * P]
                            for h2 in range(2):
                                o = dc * D + h2 * 512
                                a = te.matmul(psQ[h2][:, :], lhs, wqT[:, o:o + 512],
                                              start=(dc == 0), stop=(dc == DC - 1))
                                if dc == DC - 1:
                                    sp.inc(a)
                                a = te.matmul(psK[h2][:, :], lhs, wkT[:, o:o + 512],
                                              start=(dc == 0), stop=(dc == DC - 1))
                                if dc == DC - 1:
                                    sp.inc(a)
                                a = te.matmul(psV[h2][:, :], lhs, wvT[:, o:o + 512],
                                              start=(dc == 0), stop=(dc == DC - 1))
                                if dc == DC - 1:
                                    sp.inc(a)
                    qkv_mark[t] = sp.v

                def emit_ropeq(t):
                    @blk.vector
                    def _(v, t=t):
                        v.wait_ge(ss.h, qkbv_mark[t])
                        cs2, csm = cs_mark[t]
                        v.wait_ge(cs2.h, csm)
                        cs = cs_of(t)
                        qe = qb.ap()[:, (t % 2) * D:(t % 2 + 1) * D:2]
                        qo = qb.ap()[:, (t % 2) * D + 1:(t % 2 + 1) * D:2]
                        v.tensor_mul(rtq[:, 0:512], qe, cs[:, 0:512])
                        v.tensor_mul(rtq[:, 512:1024], qo, cs[:, 512:1024])
                        v.drain()
                        sv.inc(v.tensor_sub(qp[:, (t % 2) * D:(t % 2) * D + 512],
                                            rtq[:, 0:512], rtq[:, 512:1024]))
                        v.drain()
                        v.tensor_mul(rtq[:, 0:512], qe, cs[:, 512:1024])
                        v.tensor_mul(rtq[:, 512:1024], qo, cs[:, 0:512])
                        v.drain()
                        a = v.tensor_add(qp[:, (t % 2) * D + 512:(t % 2 + 1) * D],
                                         rtq[:, 0:512], rtq[:, 512:1024])
                        v.drain()
                        sv.inc(a)
                        a = v.scalar_tensor_tensor(out=dmp[:, :],
                                                   in0=qp[:, (t % 2) * D:(t % 2 + 1) * D],
                                                   scalar=1.0,
                                                   in1=qp[:, (t % 2) * D:(t % 2 + 1) * D],
                                                   op0=ALU.mult, op1=ALU.mult,
                                                   accum_out=q2cols[:, t:t + 1])
                        v.drain()
                        sv.inc(a)
                    ropeq_mark[t] = sv.v - 1
                    q2_mark[t] = sv.v

                def emit_ropek(t):
                    @blk.gpsimd
                    def _(gp, t=t):
                        gp.wait_ge(ss.h, qkbv_mark[t])
                        cs2, csm = cs_mark[t]
                        gp.wait_ge(cs2.h, csm)
                        cs = cs_of(t)
                        ke = kb.ap()[:, (t % 2) * D:(t % 2 + 1) * D:2]
                        ko = kb.ap()[:, (t % 2) * D + 1:(t % 2 + 1) * D:2]
                        gp.tensor_tensor(rtk[:, 0:512], ke, cs[:, 0:512], ALU.mult)
                        gp.tensor_tensor(rtk[:, 512:1024], ko, cs[:, 512:1024], ALU.mult)
                        gp.drain()
                        sg.inc(gp.tensor_tensor(kp[:, (t % 2) * D:(t % 2) * D + 512],
                                                rtk[:, 0:512], rtk[:, 512:1024], ALU.subtract))
                        gp.drain()
                        gp.tensor_tensor(rtk[:, 0:512], ke, cs[:, 512:1024], ALU.mult)
                        gp.tensor_tensor(rtk[:, 512:1024], ko, cs[:, 0:512], ALU.mult)
                        gp.drain()
                        a = gp.tensor_tensor(kp[:, (t % 2) * D + 512:(t % 2 + 1) * D],
                                             rtk[:, 0:512], rtk[:, 512:1024], ALU.add)
                        gp.drain()
                        sg.inc(a)
                    ropek_mark[t] = sg.v

                def emit_csload(t):
                    if t < 2 or t > NT:
                        return

                    @blk.gpsimd
                    def _(gp, t=t):
                        if t >= 3:
                            sc.w(gp)
                            sq.inc(gp.nop())
                        if t < NT:
                            sv.w(gp, ropeq_mark[t - 2])
                            sc.inc(gp.dma_start(out=cs_of(t)[:, :], in_=cs_h[t]), 16)
                    if t >= 3:
                        cs_mark[t - 1] = (sq, sq.v)

                def emit_k2(t):
                    @blk.vector
                    def _(v, t=t):
                        v.wait_ge(sg.h, ropek_mark[t])
                        a = v.scalar_tensor_tensor(out=dmp[:, :],
                                                   in0=kp[:, (t % 2) * D:(t % 2 + 1) * D],
                                                   scalar=1.0,
                                                   in1=kp[:, (t % 2) * D:(t % 2 + 1) * D],
                                                   op0=ALU.mult, op1=ALU.mult,
                                                   accum_out=k2cols[:, t:t + 1])
                        v.drain()
                        sv.inc(a)
                    k2_mark[t] = sv.v

                def emit_qT(t):
                    @blk.tensor
                    def _(te, t=t):
                        sv.w(te, ropeq_mark[t])
                        if t >= 1:
                            te.wait_ge(ss.h, kpT_mark[t - 1])
                        for cg in range(DC):
                            sp.inc(te.transpose(psTqk[:, cg * P:(cg + 1) * P],
                                                qp[:, (t % 2) * D + cg * P:(t % 2) * D + (cg + 1) * P],
                                                ident_b[:, :]))
                    qT_trans_mark[t] = sp.v

                def emit_kT(t):
                    qpT_mark[t] = ss.v + 1

                    @blk.tensor
                    def _(te, t=t):
                        te.wait_ge(sg.h, ropek_mark[t])
                        te.wait_ge(ss.h, qpT_mark[t])
                        for cg in range(DC):
                            sp.inc(te.transpose(psTqk[:, cg * P:(cg + 1) * P],
                                                kp[:, (t % 2) * D + cg * P:(t % 2) * D + (cg + 1) * P],
                                                ident_b[:, :]))
                    kT_trans_mark[t] = sp.v

                def emit_scalar(t):
                    @blk.scalar
                    def _(s, t=t):
                        s.wait_ge(sp.h, qT_trans_mark[t])
                        if t >= 2:
                            s.wait_ge(sk.h, store_mark.get(t - 2, 0))
                        ss.inc(s.copy(qpT[:, (t % 2) * D:(t % 2 + 1) * D], psTqk[:, :]))
                        s.wait_ge(sp.h, kT_trans_mark[t])
                        ss.inc(s.copy(kpT[:, (t % 2) * D:(t % 2 + 1) * D], psTqk[:, :]))
                        if t + 2 < NT:
                            s.wait_ge(sp.h, transA_mark[t + 2])
                            s.wait_ge(sp.h, qkv_mark[t])
                            ss.inc(s.copy(xnT[:, (t % 2) * D:(t % 2 + 1) * D], psT[:, :]))
                        if t + 1 < NT:
                            s.wait_ge(sp.h, qkv_mark[t + 1])
                            s.wait_ge(sk.h, store_mark.get(t - 1, 0))
                            if t >= 1:
                                s.wait_ge(sv.h, ropeq_mark[t])
                                s.wait_ge(sg.h, ropek_mark[t])
                            for h2 in range(2):
                                ss.inc(s.copy(qb[:, ((t + 1) % 2) * D + h2 * 512:((t + 1) % 2) * D + (h2 + 1) * 512],
                                              psQ[h2][:, :]))
                                ss.inc(s.copy(kb[:, ((t + 1) % 2) * D + h2 * 512:((t + 1) % 2) * D + (h2 + 1) * 512],
                                              psK[h2][:, :]))
                                ss.inc(s.copy(vbv[:, ((t + 1) % 2) * D + h2 * 512:((t + 1) % 2) * D + (h2 + 1) * 512],
                                              psV[h2][:, :]))
                    assert qpT_mark[t] == ss.v - (6 if t + 1 < NT else 0) - (1 if t + 2 < NT else 0) - 1
                    kpT_mark[t] = qpT_mark[t] + 1
                    if t + 2 < NT:
                        xnT_mark[t + 2] = kpT_mark[t] + 1
                    if t + 1 < NT:
                        qkbv_mark[t + 1] = ss.v

                def emit_store(t):
                    @blk.sync
                    def _(sy, t=t):
                        ss.w(sy, kpT_mark[t])
                        sv.w(sy, q2_mark[t])
                        sg.w(sy, ropek_mark[t])
                        so.inc(sy.dma_start(out=qT_h[t], in_=qpT[:, (t % 2) * D:(t % 2 + 1) * D]), 16)
                        so.inc(sy.dma_start(out=kT_h[t], in_=kpT[:, (t % 2) * D:(t % 2 + 1) * D]), 16)
                        so.inc(sy.dma_start(out=v_h[t], in_=vbv[:, (t % 2) * D:(t % 2 + 1) * D]), 16)
                        so.w(sy)
                        sk.inc(sy.nop())
                    store_mark[t] = sk.v

                emit_load(0)
                emit_stats(0)
                emit_transA(0)

                @blk.scalar
                def _(s):
                    s.wait_ge(sp.h, transA_mark[0])
                    ss.inc(s.copy(xnT[:, 0:D], psT[:, :]))
                xnT_mark[0] = ss.v
                emit_load(1)
                emit_stats(1)
                emit_transA(1)
                emit_qkv(0)

                @blk.scalar
                def _(s):
                    s.wait_ge(sp.h, transA_mark[1])
                    ss.inc(s.copy(xnT[:, D:2 * D], psT[:, :]))
                    s.wait_ge(sp.h, qkv_mark[0])
                    for h2 in range(2):
                        ss.inc(s.copy(qb[:, h2 * 512:(h2 + 1) * 512], psQ[h2][:, :]))
                        ss.inc(s.copy(kb[:, h2 * 512:(h2 + 1) * 512], psK[h2][:, :]))
                        ss.inc(s.copy(vbv[:, h2 * 512:(h2 + 1) * 512], psV[h2][:, :]))
                xnT_mark[1] = ss.v - 6
                qkbv_mark[0] = ss.v
                emit_load(2)
                emit_stats(2)

                for t in range(NT):
                    emit_ropeq(t)
                    emit_ropek(t)
                    emit_csload(t + 2)
                    emit_k2(t)
                    emit_transA(t + 2)
                    emit_qT(t)
                    emit_qkv(t + 1)
                    emit_kT(t)
                    emit_scalar(t)
                    emit_store(t)
                    emit_load(t + 3)
                    emit_stats(t + 3)

                @blk.vector
                def _(v):
                    sv.w(v)
                    a = v.reduce_max(C(9), k2cols[:, 0:NT], axis=AX)
                    v.drain()
                    sv.inc(a)

                @blk.tensor
                def _(te):
                    sv.w(te)
                    ss.w(te)
                    sp.inc(te.transpose(psQ[0][0:1, 0:P], C(9), ident_f[:, :]))
                    sp.inc(te.transpose(psK[0][0:NT, 0:P], q2cols[:, 0:NT], ident_f[:, :]))

                @blk.vector
                def _(v):
                    sp.w(v)
                    a = v.reduce_max(cols[0:1, 10:11], psQ[0][0:1, 0:P], axis=AX)
                    v.drain()
                    sv.inc(a)

                @blk.scalar
                def _(s):
                    sp.w(s)
                    ss.inc(s.copy(rtq[0:NT, 0:P], psK[0][0:NT, 0:P]))

                @blk.sync
                def _(sy):
                    ss.w(sy)
                    sd.inc(sy.dma_start(out=row32[0:1, 0:NT * P], in_=rtq[0:NT, 0:P]), 16)
                    sd.w(sy)
                    sk.inc(sy.nop())

                @blk.vector
                def _(v):
                    sk.w(v)
                    sv.w(v)
                    v.tensor_scalar(out=row32[0:1, 0:NT * P], in0=row32[0:1, 0:NT * P],
                                    scalar1=cols[0:1, 10:11], scalar2=0.5, op0=ALU.add, op1=ALU.mult)
                    v.drain()
                    a = v.tensor_copy(m32bf[0:1, 0:NT * P], row32[0:1, 0:NT * P])
                    v.drain()
                    sv.inc(a)

                @blk.sync
                def _(sy):
                    sv.w(sy)
                    so.inc(sy.dma_start(out=m32_hd[0:1, :], in_=m32bf[0:1, 0:NT * P]), 16)
                    so.w(sy)

        def attn_phase(m32bf):
            with ExitStack() as ph:
                qtg = ph.enter_context(nc.sbuf_tensor("b1_qtg", [P, 2 * D], BF16))
                x1g = ph.enter_context(nc.sbuf_tensor("b1_x1g", [P, 2 * D], F32))
                x2b = ph.enter_context(nc.sbuf_tensor("b1_x2b", [P, 2 * D], F32))
                m32g = ph.enter_context(nc.sbuf_tensor("b1_m32g", [1, (NT // 2) * P], BF16))
                cmask = ph.enter_context(nc.sbuf_tensor("b1_cmask", [P, 4 * 256], BF16))
                pcol = ph.enter_context(nc.sbuf_tensor("b1_pid", [1, 1], mybir.dt.uint32))
                kbuf = [ph.enter_context(nc.sbuf_tensor(f"b1_kbuf{i}", [P, 4 * D], BF16)) for i in range(2)]
                vbuf = [ph.enter_context(nc.sbuf_tensor(f"b1_vbuf{i}", [P, 4 * D], BF16)) for i in range(2)]
                pbuf = [ph.enter_context(nc.sbuf_tensor(f"b1_p{i}", [P, 256], BF16)) for i in range(2)]
                rcol = ph.enter_context(nc.sbuf_tensor("b1_rcol", [P, 2], F32))
                psS = [ph.enter_context(nc.psum_tensor(f"b1_psS{i}", [P, 512], F32)) for i in range(2)]
                psA = [[ph.enter_context(nc.psum_tensor(f"b1_psA{i}{h}", [P, 512], F32))
                        for h in range(2)] for i in range(2)]
                psN = [ph.enter_context(nc.psum_tensor(f"b1_psN{i}", [P, 512], F32)) for i in range(2)]
                rq = ph.enter_context(nc.sync.register("b1_rq"))
                rx = ph.enter_context(nc.sync.register("b1_rx"))
                rm = ph.enter_context(nc.sync.register("b1_rm"))
                blk = ph.enter_context(nc.Block())

                kk = kT_h.rearrange("n p d -> p n d")
                vv = v_h.rearrange("n p d -> p n d")
                qtg4 = qtg.ap().rearrange("p (i c t) -> p c i t", i=2, c=DC, t=P)
                NGB = NT // 4

                TL = P * D
                NGB0 = NT // 4

                @blk.sync
                def _(sy):
                    sd.w(sy)
                    sd.inc(sy.dma_start(out=cmask[:, :], in_=cmask_h[:, :]), 16)
                    sd.inc(sy.dma_start(out=pcol[0:1, 0:1], in_=pid_h[:, :]), 16)
                    sd.w(sy)
                    sy.reg_load(rq, pcol[0:1, 0:1])
                    sy.reg_mul(rq, rq, 2 * TL)
                    sy.reg_load(rx, pcol[0:1, 0:1])
                    sy.reg_mul(rx, rx, 2 * TL)
                    sy.reg_load(rm, pcol[0:1, 0:1])
                    sy.reg_mul(rm, rm, 2 * P)
                    sd.inc(sy.dma_start(
                        out=bass.AP(qTl_h.tensor, 0, [[2 * TL, NGB0], [1, 2 * TL]]),
                        in_=bass.AP(qT_h.tensor, rq, [[4 * TL, NGB0], [1, 2 * TL]]),
                        bounds_check=None), 16)
                    sd.inc(sy.dma_start(
                        out=bass.AP(x1l_h.tensor, 0, [[2 * TL, NGB0], [1, 2 * TL]]),
                        in_=bass.AP(x1_h.tensor, rx,
                                    [[4 * TL, NGB0], [1, 2 * TL]]),
                        bounds_check=None), 16)
                    sd.inc(sy.dma_start(
                        out=bass.AP(m32l_h.tensor, 0, [[2 * P, NGB0], [1, 2 * P]]),
                        in_=bass.AP(m32_hd.tensor, rm, [[4 * P, NGB0], [1, 2 * P]]),
                        bounds_check=None), 16)
                    sd.w(sy)
                    sd.inc(sy.dma_start(out=m32g[0:1, :], in_=m32l_h[0:1, 0:(NT // 2) * P]), 16)
                    sd.w(sy)
                    sk.inc(sy.nop())

                ss0 = ss.v
                sp0 = sp.v
                sg0 = sg.v
                sk0 = sk.v
                exp_mark = []
                pcons_mark = []
                sgm_mark = []
                batch_mark = []
                kload_mark = []
                gload_mark = []
                s_mark = []
                sval = sp0
                ssv = ss0
                sgv = sg.v
                skv = sk0
                pcons_d = {}
                batch_d = {}
                e = 0
                bgc = 0
                for gq in range(NGB):
                    jmax = 4 * gq + 3
                    skv += 1
                    gload_mark.append(skv)
                    for j in range(jmax + 1):
                        if j % 4 == 0:
                            skv += 1
                            kload_mark.append(skv)
                            bgc += 1
                        sval += 1
                        s_mark.append(sval)
                        ssv += 1
                        exp_mark.append(ssv)
                        if j >= 4 * gq:
                            sgv += 1
                        sgm_mark.append(sgv)
                        if j > 0:
                            sval += 1
                            pcons_d[e - 1] = sval
                            if (j - 1) % 4 == 3:
                                batch_d[bgc - 1 - (1 if j % 4 == 0 else 0)] = sval
                        e += 1
                    sval += 6
                    pcons_d[e - 1] = sval
                    batch_d[bgc - 1] = sval
                for i in range(len(exp_mark)):
                    pcons_mark.append(pcons_d[i])
                for i in range(bgc):
                    batch_mark.append(batch_d[i])
                e_of = []
                bg_of = []
                eacc = 0
                bacc = 0
                for gq in range(NGB):
                    e_of.append(eacc)
                    bg_of.append(bacc)
                    eacc += 4 * gq + 4
                    bacc += gq + 1

                for gq in range(NGB):
                    jmax = 4 * gq + 3
                    e0 = e_of[gq]
                    b0 = bg_of[gq]
                    nb = gq + 1

                    @blk.sync
                    def _(sy, gq=gq, jmax=jmax, b0=b0, nb=nb):
                        if gq >= 2:
                            so.w(sy)
                            sv.w(sy)
                        sd.inc(sy.dma_start(out=qtg[:, 0:D], in_=qTl_h[2 * gq]), 16)
                        sd.inc(sy.dma_start(out=qtg[:, D:2 * D], in_=qTl_h[2 * gq + 1]), 16)
                        sd.inc(sy.dma_start(out=x1g[:, 0:D], in_=x1l_h[2 * gq]), 16)
                        sd.inc(sy.dma_start(out=x1g[:, D:2 * D], in_=x1l_h[2 * gq + 1]), 16)
                        sd.w(sy)
                        sk.inc(sy.nop())
                        for bi in range(nb):
                            bgg = b0 + bi
                            if bgg >= 2:
                                sy.wait_ge(sp.h, batch_mark[bgg - 2])
                            j = bi * 4
                            n = min(4, jmax + 1 - j)
                            sd.inc(sy.dma_start(out=kbuf[bgg % 2][:, 0:n * D],
                                                in_=kk[:, j:j + n, :]), 16)
                            sd.inc(sy.dma_start(out=vbuf[bgg % 2][:, 0:n * D],
                                                in_=vv[:, j:j + n, :]), 16)
                            sd.w(sy)
                            sk.inc(sy.nop())

                    @blk.tensor
                    def _(te, gq=gq, jmax=jmax, e0=e0, b0=b0):
                        te.wait_ge(sk.h, gload_mark[gq])

                        def av(j, final):
                            eg = e0 + j
                            te.wait_ge(ss.h, exp_mark[eg])
                            if j >= 4 * gq:
                                te.wait_ge(sg.h, sgm_mark[eg])
                            vb = vbuf[(b0 + j // 4) % 2]
                            for sub in range(2):
                                lhs = pbuf[eg % 2][:, sub * P:(sub + 1) * P]
                                for h2 in range(2):
                                    a = te.matmul(psA[sub][h2][:, :], lhs,
                                                  vb[:, (j % 4) * D + h2 * 512:(j % 4) * D + (h2 + 1) * 512],
                                                  start=(j == 0), stop=final)
                                    if final:
                                        sp.inc(a)
                                a = te.matmul(psN[sub][:, 0:1], lhs, onescol_b[:, 0:1],
                                              start=(j == 0), stop=final)
                                if final:
                                    sp.inc(a)
                                elif sub == 1:
                                    sp.inc(a)
                        for j in range(jmax + 1):
                            eg = e0 + j
                            bgg = b0 + j // 4
                            te.wait_ge(sk.h, kload_mark[bgg])
                            if eg >= 2:
                                te.wait_ge(ss.h, exp_mark[eg - 2])
                            kv = kbuf[bgg % 2]
                            for dc in range(DC):
                                te.matmul(psS[eg % 2][:, 0:256],
                                          kv[:, (j % 4) * D + dc * P:(j % 4) * D + dc * P + P],
                                          qtg4[:, dc], start=(dc == 0), stop=False)
                            a = te.matmul(psS[eg % 2][:, 0:256], negones_b[0:1, :],
                                          m32g[0:1, gq * 256:(gq + 1) * 256], start=False, stop=True)
                            sp.inc(a)
                            assert sp.v == s_mark[eg]
                            if j > 0:
                                av(j - 1, False)
                        av(jmax, True)
                        assert sp.v == pcons_mark[e0 + jmax], (sp.v, pcons_mark[e0 + jmax])

                    @blk.scalar
                    def _(s, gq=gq, jmax=jmax, e0=e0):
                        for j in range(jmax + 1):
                            eg = e0 + j
                            if eg >= 2:
                                s.wait_ge(sp.h, pcons_mark[eg - 2])
                            s.wait_ge(sp.h, s_mark[eg])
                            a = s.activation(out=pbuf[eg % 2][:, :], in_=psS[eg % 2][:, 0:256],
                                             func=AF.Exp, bias=0.0, scale=1.0 / 32)
                            ss.inc(a)
                        assert ss.v == exp_mark[e0 + jmax]

                    @blk.gpsimd
                    def _(gp, gq=gq, jmax=jmax, e0=e0):
                        for j in range(4 * gq, jmax + 1):
                            eg = e0 + j
                            m = j - 4 * gq
                            gp.wait_ge(ss.h, exp_mark[eg])
                            a = gp.tensor_tensor(pbuf[eg % 2][:, :], pbuf[eg % 2][:, :],
                                                 cmask[:, m * 256:(m + 1) * 256], ALU.mult)
                            gp.drain()
                            sg.inc(a)
                        assert sg.v == sgm_mark[e0 + jmax]

                    @blk.vector
                    def _(v, gq=gq, jmax=jmax, e0=e0):
                        v.wait_ge(sp.h, pcons_mark[e0 + jmax])
                        so.w(v)
                        for sub in range(2):
                            v.tensor_scalar(out=rcol[:, sub:sub + 1], in0=psN[sub][:, 0:1],
                                            scalar1=SC21, scalar2=5e-13, op0=ALU.mult, op1=ALU.max)
                        v.drain()
                        v.reciprocal(rcol[:, 0:1], rcol[:, 0:1])
                        v.reciprocal(rcol[:, 1:2], rcol[:, 1:2])
                        v.drain()
                        v.tensor_scalar(out=rcol[:, 0:2], in0=rcol[:, 0:2], scalar1=SC21,
                                        scalar2=None, op0=ALU.mult)
                        v.drain()
                        a = None
                        for sub in range(2):
                            for h2 in range(2):
                                a = v.scalar_tensor_tensor(
                                    out=x2b[:, sub * D + h2 * 512:sub * D + (h2 + 1) * 512],
                                    in0=psA[sub][h2][:, :], scalar=rcol[:, sub:sub + 1],
                                    in1=x1g[:, sub * D + h2 * 512:sub * D + (h2 + 1) * 512],
                                    op0=ALU.mult, op1=ALU.add)
                        v.drain()
                        sv.inc(a)

                    @blk.sync
                    def _(sy, gq=gq):
                        sv.w(sy)
                        so.inc(sy.dma_start(out=x2_h[2 * gq], in_=x2b[:, 0:D]), 16)
                        so.inc(sy.dma_start(out=x2_h[2 * gq + 1], in_=x2b[:, D:2 * D]), 16)

                @blk.sync
                def _(sy):
                    so.w(sy)

        ffn_phase(x0_h, x1_h, NT, "a1")
        with ExitStack() as mid:
            m32bf = mid.enter_context(nc.sbuf_tensor("m32bf", [1, NT * P], BF16))
            qkv_phase(m32bf)
            attn_phase(m32bf)
        ffn_phase(x2_h, out_h, NT // 2, "b2")

    return nc


def kernel(L0_attn_x, L0_attn_Wq, L0_attn_Wk, L0_attn_Wv,
           L0_ffn_W_fc, L0_ffn_ln_w, L0_ffn_ln_b, L0_ffn_W_proj):
    x = np.ascontiguousarray(np.asarray(L0_attn_x, dtype=np.float32))
    B, S, d = x.shape
    NT = S // P
    nc = build_program(NT=NT)
    wq = np.ascontiguousarray(np.asarray(L0_attn_Wq, np.float32))
    wk = np.ascontiguousarray(np.asarray(L0_attn_Wk, np.float32))
    wv = np.ascontiguousarray(np.asarray(L0_attn_Wv, np.float32))
    wfc = np.ascontiguousarray(np.asarray(L0_ffn_W_fc, np.float32))
    lnw = np.ascontiguousarray(np.asarray(L0_ffn_ln_w, np.float32).reshape(HC, P))
    lnb = np.ascontiguousarray(np.asarray(L0_ffn_ln_b, np.float32).reshape(HC, P))
    wpj = np.ascontiguousarray(np.asarray(L0_ffn_W_proj, np.float32))
    import ml_dtypes
    import ctypes
    _libm = ctypes.CDLL("libm.so.6")
    _libm.sincosf.restype = None
    _libm.sincosf.argtypes = [ctypes.c_float, ctypes.POINTER(ctypes.c_float),
                              ctypes.POINTER(ctypes.c_float)]
    inv32 = (10000.0 ** (-np.arange(0, d, 2, dtype=np.float64) / d)).astype(np.float32)
    ang32 = (np.arange(S, dtype=np.float64)[:, None] * inv32.astype(np.float64)).astype(np.float32)
    ctab = np.empty(S * 512, np.float32)
    stab = np.empty(S * 512, np.float32)
    _s = ctypes.c_float()
    _c = ctypes.c_float()
    _bs, _bc = ctypes.byref(_s), ctypes.byref(_c)
    _fn = _libm.sincosf
    for _i, _x in enumerate(ang32.reshape(-1).tolist()):
        _fn(_x, _bs, _bc)
        stab[_i] = _s.value
        ctab[_i] = _c.value
    cstab = np.concatenate([ctab.reshape(NT, P, 512), stab.reshape(NT, P, 512)],
                           axis=2)
    kkq = np.arange(256)[None, :]
    kkk = np.arange(P)[:, None]
    cmasks = []
    for pi in range(2):
        blocks = [(kkq - kkk + (2 * pi - m) * P >= 0).astype(np.float32) for m in range(4)]
        cmasks.append(np.concatenate(blocks, axis=1).astype(ml_dtypes.bfloat16))
    in_maps = []
    for c in range(8):
        b = c // 2
        in_maps.append({
            "xseq": np.ascontiguousarray(x[b].reshape(NT, P, d)),
            "Wq": wq, "Wk": wk, "Wv": wv, "Wfc": wfc,
            "lnw": lnw, "lnb": lnb, "Wproj": wpj,
            "pid": np.array([[c % 2]], np.uint32),
            "cmask": cmasks[c % 2],
            "cstab": cstab,
        })
    res = run_bass_kernel_spmd(nc, in_maps, core_ids=list(range(8)))
    out = np.empty((B, S, d), np.asarray(res.results[0]["out"]).dtype)
    for b in range(B):
        for c2, pi in ((2 * b, 0), (2 * b + 1, 1)):
            o = np.asarray(res.results[c2]["out"]).reshape(NT // 2, P, d)
            for i in range(NT // 2):
                gtile = 2 * (i - i % 2) + 2 * pi + i % 2
                out[b, gtile * P:(gtile + 1) * P] = o[i]
    return out


# version 21 (final)
# speedup vs baseline: 1.0207x; 1.0207x over previous
import numpy as np
from contextlib import ExitStack

import concourse.bass as bass
import concourse.mybir as mybir
from concourse.bass_utils import run_bass_kernel_spmd

F32 = mybir.dt.float32
BF16 = mybir.dt.bfloat16
AF = mybir.ActivationFunctionType
ALU = mybir.AluOpType
AX = mybir.AxisListType.X

D = 1024
HQ = 4096
P = 128
DC = D // P
HC = HQ // P
LN_EPS = 1e-5
ABS_EPS = 1e-6
L_LN = float(np.log(10000.0))
L1 = float(np.float32(L_LN))
L2 = float(np.float32(L_LN - L1))
SC21 = 2097152.0


class Ctr:
    def __init__(self, h):
        self.h = h
        self.v = 0

    def inc(self, inst, n=1):
        inst.then_inc(self.h, n)
        self.v += n

    def w(self, eng, v=None):
        eng.wait_ge(self.h, self.v if v is None else v)


def newton_rsqrt(v, r, vneg, t, steps):
    for _ in range(steps):
        v.scalar_tensor_tensor(out=t, in0=r, scalar=vneg, in1=r, op0=ALU.mult, op1=ALU.mult)
        v.drain()
        v.scalar_tensor_tensor(out=r, in0=t, scalar=1.5, in1=r, op0=ALU.add, op1=ALU.mult)
        v.drain()


def rsqrt_col(v, rcol, vcol, tcol, seed_k, seed_c, vneg):
    v.tensor_scalar(out=vneg, in0=vcol, scalar1=-0.5, scalar2=None, op0=ALU.mult)
    v.reciprocal(rcol, vcol)
    v.drain()
    v.tensor_scalar(out=rcol, in0=rcol, scalar1=seed_k, scalar2=seed_c, op0=ALU.mult, op1=ALU.min)
    v.drain()
    newton_rsqrt(v, rcol, vneg, tcol, 5)


def build_program(NT=64, debug=False):
    nc = bass.Bass("TRN2", target_bir_lowering=False, debug=False)
    NG = NT // 2

    kd = "ExternalOutput" if debug else "Internal"
    x0_h = nc.dram_tensor("xseq", [NT, P, D], F32, kind="ExternalInput").ap()
    wq_h = nc.dram_tensor("Wq", [D, D], F32, kind="ExternalInput").ap()
    wk_h = nc.dram_tensor("Wk", [D, D], F32, kind="ExternalInput").ap()
    wv_h = nc.dram_tensor("Wv", [D, D], F32, kind="ExternalInput").ap()
    wfc_h = nc.dram_tensor("Wfc", [HQ, D], F32, kind="ExternalInput").ap()
    lnw_h = nc.dram_tensor("lnw", [HC, P], F32, kind="ExternalInput").ap()
    lnb_h = nc.dram_tensor("lnb", [HC, P], F32, kind="ExternalInput").ap()
    wpj_h = nc.dram_tensor("Wproj", [D, HQ], F32, kind="ExternalInput").ap()
    out_h = nc.dram_tensor("out", [NT // 2, P, D], F32, kind="ExternalOutput").ap()
    pid_h = nc.dram_tensor("pid", [1, 1], mybir.dt.uint32, kind="ExternalInput").ap()
    cmask_h = nc.dram_tensor("cmask", [P, 4 * 256], BF16, kind="ExternalInput").ap()
    cs_h = nc.dram_tensor("cstab", [NT, P, 1024], F32, kind="ExternalInput").ap()

    x1_h = nc.dram_tensor("x1s", [NT, P, D], F32, kind=kd).ap()
    x2_h = nc.dram_tensor("x2s", [NT // 2, P, D], F32, kind=kd).ap()
    m32_hd = nc.dram_tensor("m32s", [1, NT * P], BF16, kind=kd).ap()
    qTl_h = nc.dram_tensor("qTl", [NT // 2, P, D], BF16, kind="Internal").ap()
    x1l_h = nc.dram_tensor("x1l", [NT // 2, P, D], F32, kind="Internal").ap()
    m32l_h = nc.dram_tensor("m32l", [1, (NT // 2) * P], BF16, kind="Internal").ap()
    kT_h = nc.dram_tensor("kTs", [NT, P, D], BF16, kind=kd).ap()
    qT_h = nc.dram_tensor("qTs", [NT, P, D], BF16, kind=kd).ap()
    v_h = nc.dram_tensor("vs", [NT, P, D], BF16, kind=kd).ap()
    wfcT_h = nc.dram_tensor("wfcTs", [P, DC * HQ], BF16, kind=kd).ap()
    wpjT_h = nc.dram_tensor("wpjTs", [P, HC * D], BF16, kind=kd).ap()
    wqT_h = nc.dram_tensor("wqTs", [P, DC * D], BF16, kind=kd).ap()
    wkT_h = nc.dram_tensor("wkTs", [P, DC * D], BF16, kind=kd).ap()
    wvT_h = nc.dram_tensor("wvTs", [P, DC * D], BF16, kind=kd).ap()

    with ExitStack() as top:
        ident_b = top.enter_context(nc.sbuf_tensor("ident_b", [P, P], BF16))
        ident_f = top.enter_context(nc.sbuf_tensor("ident_f", [P, P], F32))
        ones_f = top.enter_context(nc.sbuf_tensor("ones_f", [1, P], F32))
        negones_b = top.enter_context(nc.sbuf_tensor("negones_b", [1, P], BF16))
        onescol_b = top.enter_context(nc.sbuf_tensor("onescol_b", [P, 1], BF16))
        cbh = top.enter_context(nc.sbuf_tensor("cbh", [P, D], F32))
        ct = top.enter_context(nc.sbuf_tensor("ct", [P, 512], F32))
        st = top.enter_context(nc.sbuf_tensor("st", [P, 512], F32))
        rbc = top.enter_context(nc.sbuf_tensor("rbc", [P, 512], F32))
        rbs = top.enter_context(nc.sbuf_tensor("rbs", [P, 512], F32))
        rbc2 = top.enter_context(nc.sbuf_tensor("rbc2", [P, 512], F32))
        rbs2 = top.enter_context(nc.sbuf_tensor("rbs2", [P, 512], F32))
        q2cols = top.enter_context(nc.sbuf_tensor("q2cols", [P, NT], F32))
        k2cols = top.enter_context(nc.sbuf_tensor("k2cols", [P, NT], F32))
        cols = top.enter_context(nc.sbuf_tensor("cols", [P, 16], F32))
        gsum = top.enter_context(nc.sbuf_tensor("gsum", [P, 8], F32))

        sd = Ctr(top.enter_context(nc.semaphore("sd")))
        so = Ctr(top.enter_context(nc.semaphore("so")))
        sp = Ctr(top.enter_context(nc.semaphore("sp")))
        sv = Ctr(top.enter_context(nc.semaphore("sv")))
        ss = Ctr(top.enter_context(nc.semaphore("ss")))
        sg = Ctr(top.enter_context(nc.semaphore("sg")))
        sk = Ctr(top.enter_context(nc.semaphore("sk")))
        sc = Ctr(top.enter_context(nc.semaphore("sc")))
        sq = Ctr(top.enter_context(nc.semaphore("sq")))

        def C(i):
            return cols[:, i:i + 1]

        with ExitStack() as ph:
            pp_a = ph.enter_context(nc.psum_tensor("pp_a", [P, 512], F32))
            pp_b = ph.enter_context(nc.psum_tensor("pp_b", [P, 512], F32))
            pp_t = ph.enter_context(nc.psum_tensor("pp_t", [P, 1024], BF16))
            pp_c = ph.enter_context(nc.psum_tensor("pp_c", [P, 512], F32))
            pp_d = ph.enter_context(nc.psum_tensor("pp_d", [P, 512], F32))
            wstr = ph.enter_context(nc.sbuf_tensor("wstr", [P, HQ], F32))
            wstr2 = ph.enter_context(nc.sbuf_tensor("wstr2", [P, HQ], F32))
            g1 = ph.enter_context(nc.sbuf_tensor("g1", [P, HQ], BF16))
            g2 = ph.enter_context(nc.sbuf_tensor("g2", [P, HQ], BF16))
            wqn = ph.enter_context(nc.sbuf_tensor("wqn", [P, HQ], BF16))
            wimg = ph.enter_context(nc.sbuf_tensor("wimg", [P, DC * HQ], BF16))
            acol = ph.enter_context(nc.sbuf_tensor("acol", [P, 34], F32))
            wcols = ph.enter_context(nc.sbuf_tensor("wcols", [P, HC], F32))
            bcols = ph.enter_context(nc.sbuf_tensor("bcols", [P, HC], BF16))
            wrow = ph.enter_context(nc.sbuf_tensor("wrow", [HC, P], F32))
            rowb = ph.enter_context(nc.sbuf_tensor("rowb", [1, 2048], F32))
            jrow = ph.enter_context(nc.sbuf_tensor("jrow", [1, 512], F32))
            e2r = ph.enter_context(nc.sbuf_tensor("e2r", [1, 512], F32))
            rot = ph.enter_context(nc.sbuf_tensor("rot", [1, 4 * 512], F32))
            rtm = ph.enter_context(nc.sbuf_tensor("rtm", [1, 3 * 512], F32))
            tA = ph.enter_context(nc.sbuf_tensor("tA", [64, 512], F32))
            tB = ph.enter_context(nc.sbuf_tensor("tB", [64, 512], F32))
            tC = ph.enter_context(nc.sbuf_tensor("tC", [64, 512], F32))
            tD = ph.enter_context(nc.sbuf_tensor("tD", [64, 512], F32))
            c2 = ph.enter_context(nc.sbuf_tensor("c2", [64, 512], F32))
            s2 = ph.enter_context(nc.sbuf_tensor("s2", [64, 512], F32))

            def rc(k):
                return rot[0:1, (k % 2) * 1024:(k % 2) * 1024 + 512]

            def rs_(k):
                return rot[0:1, (k % 2) * 1024 + 512:(k % 2) * 1024 + 1024]

            blk = ph.enter_context(nc.Block())

            @blk.gpsimd
            def _(gp):
                gp.memset(ident_b[:, :], 1.0)
                gp.memset(ident_f[:, :], 1.0)
                gp.drain()
                sg.inc(gp.affine_select(out=ident_b[:, :], in_=ident_b[:, :],
                                        compare_op=ALU.is_equal, fill=0.0, base=0,
                                        pattern=[[-1, P]], channel_multiplier=1))
                sg.inc(gp.affine_select(out=ident_f[:, :], in_=ident_f[:, :],
                                        compare_op=ALU.is_equal, fill=0.0, base=0,
                                        pattern=[[-1, P]], channel_multiplier=1))
                sg.inc(gp.memset(ones_f[0:1, :], 1.0))
                sg.inc(gp.memset(negones_b[0:1, :], -1.0))
                sg.inc(gp.memset(onescol_b[:, 0:1], 1.0))
                sg.inc(gp.iota(jrow[0:1, :], [[1, 512]], base=0, channel_multiplier=0,
                               allow_small_or_imprecise_dtypes=True))
                sg.inc(gp.memset(ct[0:1, :], 1.0))
                sg.inc(gp.memset(st[0:1, :], 0.0))

            @blk.sync
            def _(sy):
                sd.w(sy)
                sd.inc(sy.dma_start(out=wrow[0:HC, :], in_=lnw_h[:, :]), 16)
                sd.w(sy)
                sk.inc(sy.nop())

            @blk.tensor
            def _(te):
                sk.w(te)
                sg.w(te)
                ss.w(te)
                sp.inc(te.transpose(pp_a[:, 0:HC], wrow[0:HC, :], ident_f[0:HC, 0:HC]))

            @blk.vector
            def _(v):
                sp.w(v)
                a = v.tensor_scalar(out=wcols[:, :], in0=pp_a[:, 0:HC], scalar1=0.5,
                                    scalar2=None, op0=ALU.mult)
                v.drain()
                sv.inc(a)

            @blk.sync
            def _(sy):
                sv.w(sy)
                sd.inc(sy.dma_start(out=wrow[0:HC, :], in_=lnb_h[:, :]), 16)
                sd.w(sy)
                sk.inc(sy.nop())

            @blk.tensor
            def _(te):
                sk.w(te)
                sp.inc(te.transpose(pp_a[:, 0:HC], wrow[0:HC, :], ident_f[0:HC, 0:HC]))

            @blk.scalar
            def _(s):
                sp.w(s)
                ss.inc(s.copy(bcols[:, :], pp_a[:, 0:HC]))

            for wi, (src, n_tile, tile_w, img_dst, img_w) in enumerate(
                    [(wfc_h, HC, D, wfcT_h, DC * HQ), (wpj_h, DC, HQ, wpjT_h, HC * D)]):
                src_t = src.rearrange("(n p) w -> n p w", p=P)
                for r in range(n_tile):
                    @blk.sync
                    def _(sy, r=r, src_t=src_t, tile_w=tile_w):
                        if r >= 2:
                            sv.w(sy)
                        buf = wstr if r % 2 == 0 else wstr2
                        sd.inc(sy.dma_start(out=buf[:, 0:tile_w], in_=src_t[r]), 16)
                        sd.w(sy)
                        sk.inc(sy.nop())

                    @blk.vector
                    def _(v, r=r, tile_w=tile_w):
                        sk.w(v)
                        buf = wstr if r % 2 == 0 else wstr2
                        a = v.reduce_sum(acol[:, r:r + 1], buf[:, 0:tile_w], axis=AX,
                                         apply_absolute_value=True)
                        v.drain()
                        sv.inc(a)

                @blk.vector
                def _(v, n_tile=n_tile):
                    a = v.reduce_sum(C(0), acol[:, 0:n_tile], axis=AX)
                    v.drain()
                    sv.inc(a)

                @blk.tensor
                def _(te):
                    sv.w(te)
                    sp.inc(te.transpose(pp_c[0:1, 0:P], C(0), ident_f[:, :]))

                @blk.vector
                def _(v):
                    sp.w(v)
                    v.reduce_sum(rowb[0:1, 0:1], pp_c[0:1, 0:P], axis=AX)
                    v.drain()
                    v.tensor_scalar(out=rowb[0:1, 0:1], in0=rowb[0:1, 0:1],
                                    scalar1=1.0 / (HQ * D), scalar2=ABS_EPS,
                                    op0=ALU.mult, op1=ALU.add)
                    v.drain()
                    a = v.tensor_scalar(out=rowb[0:1, 1:2], in0=rowb[0:1, 0:1],
                                        scalar1=0.5, scalar2=None, op0=ALU.mult)
                    v.drain()
                    sv.inc(a)

                @blk.tensor
                def _(te):
                    sv.w(te)
                    sp.inc(te.matmul(pp_c[:, 0:1], ones_f[0:1, :], rowb[0:1, 1:2],
                                     start=True, stop=True))

                @blk.vector
                def _(v):
                    sp.w(v)
                    v.tensor_copy(C(1), pp_c[:, 0:1])
                    v.drain()
                    a = v.tensor_scalar(out=C(2), in0=C(1), scalar1=-1.0, scalar2=None, op0=ALU.mult)
                    v.drain()
                    sv.inc(a)

                nfc = tile_w // P
                for r in range(n_tile):
                    @blk.sync
                    def _(sy, r=r, src_t=src_t, tile_w=tile_w):
                        sv.w(sy)
                        buf = wstr if r % 2 == 0 else wstr2
                        sd.inc(sy.dma_start(out=buf[:, 0:tile_w], in_=src_t[r]), 16)
                        sd.w(sy)
                        sk.inc(sy.nop())

                    @blk.vector
                    def _(v, r=r, tile_w=tile_w):
                        sk.w(v)
                        sp.w(v)
                        buf = wstr if r % 2 == 0 else wstr2
                        v.tensor_scalar(out=g1[:, 0:tile_w], in0=buf[:, 0:tile_w], scalar1=C(1),
                                        scalar2=None, op0=ALU.is_gt)
                        v.tensor_scalar(out=g2[:, 0:tile_w], in0=buf[:, 0:tile_w], scalar1=C(2),
                                        scalar2=None, op0=ALU.is_lt)
                        v.drain()
                        a = v.tensor_sub(wqn[:, 0:tile_w], g1[:, 0:tile_w], g2[:, 0:tile_w])
                        v.drain()
                        sv.inc(a)

                    for cg0 in range(0, nfc, 8):
                        @blk.tensor
                        def _(te, r=r, cg0=cg0):
                            sv.w(te)
                            ss.w(te)
                            for k in range(min(8, nfc - cg0)):
                                sp.inc(te.transpose(pp_t[:, k * P:(k + 1) * P],
                                                    wqn[:, (cg0 + k) * P:(cg0 + k + 1) * P],
                                                    ident_b[:, :]))

                        @blk.scalar
                        def _(s, r=r, cg0=cg0, wi=wi):
                            sp.w(s)
                            for k in range(min(8, nfc - cg0)):
                                cg = cg0 + k
                                if wi == 0:
                                    dst = wimg[:, cg * HQ + r * P: cg * HQ + r * P + P]
                                else:
                                    dst = wimg[:, cg * D + r * P: cg * D + r * P + P]
                                ss.inc(s.copy(dst, pp_t[:, k * P:(k + 1) * P]))

                if wi == 1:
                    @blk.tensor
                    def _(te):
                        ss.w(te)
                        for hc in range(HC):
                            a = te.matmul(pp_c[0:1, 0:512], bcols[:, hc:hc + 1],
                                          wimg[:, hc * D:hc * D + 512],
                                          start=(hc == 0), stop=(hc == HC - 1))
                            a2 = te.matmul(pp_d[0:1, 0:512], bcols[:, hc:hc + 1],
                                           wimg[:, hc * D + 512:hc * D + 1024],
                                           start=(hc == 0), stop=(hc == HC - 1))
                        sp.inc(a)
                        sp.inc(a2)

                    @blk.vector
                    def _(v):
                        sp.w(v)
                        v.tensor_scalar(out=rowb[0:1, 0:512], in0=pp_c[0:1, 0:512],
                                        scalar1=0.5, scalar2=None, op0=ALU.mult)
                        a = v.tensor_scalar(out=rowb[0:1, 512:1024], in0=pp_d[0:1, 0:512],
                                            scalar1=0.5, scalar2=None, op0=ALU.mult)
                        v.drain()
                        sv.inc(a)
                        for hc in range(HC):
                            a = v.tensor_scalar(out=wimg[:, hc * D:(hc + 1) * D],
                                                in0=wimg[:, hc * D:(hc + 1) * D],
                                                scalar1=wcols[:, hc:hc + 1], scalar2=None,
                                                op0=ALU.mult)
                            sv.inc(a)
                        v.drain()

                    @blk.tensor
                    def _(te):
                        sv.w(te)
                        sp.inc(te.matmul(pp_a[:, :], ones_f[0:1, :], rowb[0:1, 0:512],
                                         start=True, stop=True))
                        sp.inc(te.matmul(pp_b[:, :], ones_f[0:1, :], rowb[0:1, 512:1024],
                                         start=True, stop=True))

                    @blk.scalar
                    def _(s):
                        sp.w(s)
                        ss.inc(s.copy(cbh[:, 0:512], pp_a[:, :]))
                        ss.inc(s.copy(cbh[:, 512:1024], pp_b[:, :]))

                @blk.sync
                def _(sy, img_dst=img_dst, img_w=img_w):
                    ss.w(sy)
                    sv.w(sy)
                    so.inc(sy.dma_start(out=img_dst[:, 0:img_w], in_=wimg[:, 0:img_w]), 16)
                    so.w(sy)

            for src, img_dst in [(wq_h, wqT_h), (wk_h, wkT_h), (wv_h, wvT_h)]:
                src_t = src.rearrange("(n p) w -> n p w", p=P)
                for r in range(DC):
                    @blk.sync
                    def _(sy, r=r, src_t=src_t):
                        if r >= 2:
                            sv.w(sy)
                        buf = wstr if r % 2 == 0 else wstr2
                        sd.inc(sy.dma_start(out=buf[:, 0:D], in_=src_t[r]), 16)
                        sd.w(sy)
                        sk.inc(sy.nop())

                    @blk.vector
                    def _(v, r=r):
                        sk.w(v)
                        sp.w(v)
                        buf = wstr if r % 2 == 0 else wstr2
                        a = v.tensor_copy(wqn[:, 0:D], buf[:, 0:D])
                        v.drain()
                        sv.inc(a)

                    @blk.tensor
                    def _(te, r=r):
                        sv.w(te)
                        ss.w(te)
                        for cg in range(DC):
                            sp.inc(te.transpose(pp_t[:, cg * P:(cg + 1) * P],
                                                wqn[:, cg * P:(cg + 1) * P], ident_b[:, :]))

                    @blk.scalar
                    def _(s, r=r):
                        sp.w(s)
                        for cg in range(DC):
                            ss.inc(s.copy(wimg[:, cg * D + r * P: cg * D + r * P + P],
                                          pp_t[:, cg * P:(cg + 1) * P]))

                @blk.sync
                def _(sy, img_dst=img_dst):
                    ss.w(sy)
                    so.inc(sy.dma_start(out=img_dst[:, 0:DC * D], in_=wimg[:, 0:DC * D]), 16)
                    so.w(sy)

        def ffn_phase(src_ap, dst_ap, n_tiles, name):
            seed_kc = (1.0, 1.0) if name == "a1" else (32.0, 0.05)
            with ExitStack() as ph:
                wfcT = ph.enter_context(nc.sbuf_tensor(f"{name}_wfcT", [P, DC * HQ], BF16))
                wpjT = ph.enter_context(nc.sbuf_tensor(f"{name}_wpjT", [P, HC * D], BF16))
                xb = ph.enter_context(nc.sbuf_tensor(f"{name}_xb", [P, 3 * D], F32))
                xcb = ph.enter_context(nc.sbuf_tensor(f"{name}_xcb", [P, 2 * D], BF16))
                xcT = ph.enter_context(nc.sbuf_tensor(f"{name}_xcT", [P, 2 * D], BF16))
                g = ph.enter_context(nc.sbuf_tensor(f"{name}_g", [P, HQ], BF16))
                gb = ph.enter_context(nc.sbuf_tensor(f"{name}_gb", [P, HQ], BF16))
                gbT = ph.enter_context(nc.sbuf_tensor(f"{name}_gbT", [P, HQ], BF16))
                dump = ph.enter_context(nc.sbuf_tensor(f"{name}_dump", [P, HQ], BF16))
                x1b = ph.enter_context(nc.sbuf_tensor(f"{name}_x1b", [P, 2 * D], F32))
                psT = [ph.enter_context(nc.psum_tensor(f"{name}_psT{i}", [P, 1024], BF16)) for i in range(2)]
                psfc = [ph.enter_context(nc.psum_tensor(f"{name}_psfc{i}", [P, 512], F32)) for i in range(4)]
                pp = [ph.enter_context(nc.psum_tensor(f"{name}_pp{i}", [P, 512], F32)) for i in range(2)]
                blk = ph.enter_context(nc.Block())

                @blk.sync
                def _(sy):
                    sd.w(sy)
                    sd.inc(sy.dma_start(out=wfcT[:, :], in_=wfcT_h[:, :]), 16)
                    sd.inc(sy.dma_start(out=wpjT[:, :], in_=wpjT_h[:, :]), 16)
                    sd.w(sy)
                    sk.inc(sy.nop())

                wt_ready = sk.v
                ss_base0 = ss.v
                load_mark = {}
                transA_mark = {}
                xcb_mark = {}
                rstd1_mark = {}
                evacT_mark = {}
                fc_marks = {}
                gelu_mark = {}
                gb_mark = {}
                rstd2_mark = {}
                transB_marks = {}
                transB_done = {}
                proj_mark = {}
                x1_done = {-1: sv.v}
                pool = {"i": 0, "hist": []}

                def pool_take():
                    slot = pool["i"] % 2
                    pool["i"] += 1
                    for s2, mk in reversed(pool["hist"]):
                        if s2 == slot:
                            return slot, mk
                    return slot, (ss, 0)

                def emit_load(t):
                    if t >= n_tiles:
                        return

                    @blk.sync
                    def _(sy, t=t):
                        if t >= 3:
                            sv.w(sy, x1_done[t - 3])
                            so.w(sy)
                        sd.inc(sy.dma_start(out=xb[:, (t % 3) * D:(t % 3 + 1) * D], in_=src_ap[t]), 16)
                        sd.w(sy)
                        sk.inc(sy.nop())
                    load_mark[t] = sk.v

                def emit_stats(t):
                    if t >= n_tiles:
                        return

                    @blk.vector
                    def _(v, t=t):
                        v.wait_ge(sk.h, load_mark[t])
                        if t >= 2:
                            v.wait_ge(sp.h, transA_mark[t - 2])
                            v.wait_ge(ss.h, gelu_mark[t - 2])
                        x = xb[:, (t % 3) * D:(t % 3 + 1) * D]
                        xc = xcb[:, (t % 2) * D:(t % 2 + 1) * D]
                        v.reduce_sum(C(0), x, axis=AX)
                        v.drain()
                        v.tensor_scalar(out=C(0), in0=C(0), scalar1=1.0 / D, scalar2=None, op0=ALU.mult)
                        v.drain()
                        a = v.tensor_scalar(out=xc, in0=x, scalar1=C(0), scalar2=None, op0=ALU.subtract)
                        v.drain()
                        sv.inc(a)
                        v.scalar_tensor_tensor(out=dump[:, 0:D], in0=xc, scalar=1.0, in1=xc,
                                               op0=ALU.mult, op1=ALU.mult, accum_out=C(1))
                        v.drain()
                        v.tensor_scalar(out=C(2), in0=C(1), scalar1=1.0 / D, scalar2=LN_EPS,
                                        op0=ALU.mult, op1=ALU.add)
                        v.drain()
                        rsqrt_col(v, C(3), C(2), C(6), *seed_kc, C(11))
                        a = v.tensor_copy(cols[:, 7 + (t % 2):8 + (t % 2)], C(3))
                        v.drain()
                        sv.inc(a)
                    xcb_mark[t] = sv.v - 1
                    rstd1_mark[t] = sv.v

                def emit_gb(t):
                    @blk.vector
                    def _(v, t=t):
                        v.wait_ge(ss.h, gelu_mark[t])
                        if t >= 1:
                            v.wait_ge(sp.h, transB_done[t - 1])
                        v.reduce_sum(C(4), gsum[:, 0:8], axis=AX)
                        v.drain()
                        v.tensor_scalar(out=C(4), in0=C(4), scalar1=1.0 / HQ, scalar2=None, op0=ALU.mult)
                        v.drain()
                        a = v.tensor_scalar(out=gb[:, :], in0=g[:, :], scalar1=C(4), scalar2=None,
                                            op0=ALU.subtract)
                        v.drain()
                        sv.inc(a)
                        v.scalar_tensor_tensor(out=dump[:, :], in0=gb[:, :], scalar=1.0, in1=gb[:, :],
                                               op0=ALU.mult, op1=ALU.mult, accum_out=C(1))
                        v.drain()
                        v.tensor_scalar(out=C(2), in0=C(1), scalar1=1.0 / HQ, scalar2=LN_EPS,
                                        op0=ALU.mult, op1=ALU.add)
                        v.drain()
                        rsqrt_col(v, C(5), C(2), C(6), 16.0, 0.08, C(11))
                        a = v.tensor_copy(cols[:, 9 + (t % 2):10 + (t % 2)], C(5))
                        v.drain()
                        sv.inc(a)
                    gb_mark[t] = sv.v - 1
                    rstd2_mark[t] = sv.v

                def emit_transA(t):
                    if t >= n_tiles:
                        return
                    slot, (gsem, gmk) = pool_take()

                    @blk.tensor
                    def _(te, t=t, slot=slot, gsem=gsem, gmk=gmk):
                        sv.w(te, xcb_mark[t])
                        te.wait_ge(gsem.h, gmk)
                        for cg in range(DC):
                            sp.inc(te.transpose(psT[slot][:, cg * P:(cg + 1) * P],
                                                xcb[:, (t % 2) * D + cg * P:(t % 2) * D + (cg + 1) * P],
                                                ident_b[:, :]))
                    transA_mark[t] = sp.v
                    pool["hist"].append((slot, (ss, None)))
                    return slot

                def emit_transB(t):
                    slots = []
                    for grp in range(4):
                        slot, (gsem, gmk) = pool_take()

                        @blk.tensor
                        def _(te, t=t, grp=grp, slot=slot, gsem=gsem, gmk=gmk):
                            sv.w(te, gb_mark[t])
                            te.wait_ge(gsem.h, gmk)
                            for k in range(8):
                                hc = grp * 8 + k
                                sp.inc(te.transpose(psT[slot][:, k * P:(k + 1) * P],
                                                    gb[:, hc * P:(hc + 1) * P], ident_b[:, :]))
                        transB_marks.setdefault(t, []).append(sp.v)
                        pool["hist"].append((slot, (ss, None)))
                        slots.append(slot)
                    transB_done[t] = sp.v
                    return slots

                def emit_fc(t):
                    if t >= n_tiles:
                        return
                    ms = []
                    for q in range(4):
                        @blk.tensor
                        def _(te, t=t, q=q):
                            te.wait_ge(sv.h, evacT_mark[t])
                            gm = ss_base0 + 8 * t
                            if q < 2:
                                if t >= 1:
                                    te.wait_ge(ss.h, gm - 2 + 2 * q)
                            else:
                                te.wait_ge(ss.h, gm + 2 * (q - 1))
                            te.wait_ge(sk.h, wt_ready)
                            for dc in range(DC):
                                for hg2 in range(2):
                                    off = dc * HQ + q * 1024 + hg2 * 512
                                    a = te.matmul(psfc[2 * (q % 2) + hg2][:, :],
                                                  xcT[:, (t % 2) * D + dc * P:(t % 2) * D + (dc + 1) * P],
                                                  wfcT[:, off:off + 512],
                                                  start=(dc == 0), stop=(dc == DC - 1))
                                    if dc == DC - 1:
                                        sp.inc(a)
                        ms.append(sp.v)
                    fc_marks[t] = ms

                def emit_proj(t):
                    @blk.tensor
                    def _(te, t=t):
                        sv2, m2 = bevac_mark[t]
                        te.wait_ge(sv2.h, m2)
                        te.wait_ge(sv.h, x1_done[t - 1])
                        for hc in range(HC):
                            a = te.matmul(pp[0][:, :], gbT[:, hc * P:(hc + 1) * P],
                                          wpjT[:, hc * D:hc * D + 512],
                                          start=(hc == 0), stop=(hc == HC - 1))
                            a2 = te.matmul(pp[1][:, :], gbT[:, hc * P:(hc + 1) * P],
                                           wpjT[:, hc * D + 512:hc * D + 1024],
                                           start=(hc == 0), stop=(hc == HC - 1))
                        sp.inc(a)
                        sp.inc(a2)
                    proj_mark[t] = sp.v

                bevac_mark = {}
                tB_map = {}

                def emit_evacT(t):
                    if t >= n_tiles:
                        return

                    @blk.vector
                    def _(v, t=t):
                        v.wait_ge(sp.h, transA_mark[t])
                        a = v.tensor_copy(xcT[:, (t % 2) * D:(t % 2 + 1) * D],
                                          psT[tA_map[t]][:, :])
                        v.drain()
                        sv.inc(a)
                    evacT_mark[t] = sv.v
                    for i2 in range(len(pool["hist"]) - 1, -1, -1):
                        if pool["hist"][i2][1][1] is None and pool["hist"][i2][0] == tA_map[t]:
                            pool["hist"][i2] = (tA_map[t], (sv, sv.v))
                            break

                def emit_transB_half(t, half):
                    slots = []
                    for grp in (0, 1) if half == 0 else (2, 3):
                        slot, (gsem, gmk) = pool_take()

                        @blk.tensor
                        def _(te, t=t, grp=grp, slot=slot, gsem=gsem, gmk=gmk):
                            sv.w(te, gb_mark[t])
                            te.wait_ge(gsem.h, gmk)
                            for k in range(8):
                                hc = grp * 8 + k
                                sp.inc(te.transpose(psT[slot][:, k * P:(k + 1) * P],
                                                    gb[:, hc * P:(hc + 1) * P], ident_b[:, :]))
                        transB_marks.setdefault(t, []).append(sp.v)
                        pool["hist"].append((slot, (sv, None)))
                        slots.append(slot)
                    return slots

                def emit_bevacs_half(t, half, slots):
                    @blk.vector
                    def _(v, t=t, half=half, slots=slots):
                        if half == 0 and t >= 1:
                            v.wait_ge(sp.h, proj_mark[t - 1])
                        for i3, grp in enumerate((0, 1) if half == 0 else (2, 3)):
                            v.wait_ge(sp.h, transB_marks[t][grp])
                            a = v.tensor_copy(gbT[:, grp * 1024:(grp + 1) * 1024],
                                              psT[slots[i3]][:, :])
                            v.drain()
                            sv.inc(a)
                    for i3, grp in enumerate((0, 1) if half == 0 else (2, 3)):
                        mk = sv.v - (1 - i3)
                        for i2 in range(len(pool["hist"])):
                            if pool["hist"][i2][1][1] is None and pool["hist"][i2][0] == slots[i3]:
                                pool["hist"][i2] = (slots[i3], (sv, mk))
                                break
                    if half == 1:
                        bevac_mark[t] = (sv, sv.v)

                def emit_gelu(t):
                    if t >= n_tiles:
                        return

                    @blk.scalar
                    def _(s, t=t):
                        s.wait_ge(sv.h, rstd1_mark[t])
                        if t >= 1:
                            s.wait_ge(sv.h, gb_mark[t - 1] + 1)
                        rs = cols[:, 7 + (t % 2):8 + (t % 2)]
                        for q in range(4):
                            s.wait_ge(sp.h, fc_marks[t][q])
                            for hg2 in range(2):
                                off = q * 1024 + hg2 * 512
                                ss.inc(s.activation(out=g[:, off:off + 512],
                                                    in_=psfc[2 * (q % 2) + hg2][:, :],
                                                    func=AF.Gelu, bias=0.0, scale=rs,
                                                    accum_out=gsum[:, q * 2 + hg2:q * 2 + hg2 + 1]))
                    gelu_mark[t] = ss.v

                x1_store_mark = {}

                def emit_x1(t):
                    if t < 0:
                        return

                    @blk.vector
                    def _(v, t=t):
                        v.wait_ge(sp.h, proj_mark[t])
                        v.wait_ge(sv.h, rstd2_mark[t])
                        if t >= 2:
                            v.wait_ge(sk.h, x1_store_mark[t - 2])
                        x = xb[:, (t % 3) * D:(t % 3 + 1) * D]
                        rs = cols[:, 9 + (t % 2):10 + (t % 2)]
                        for h2 in range(2):
                            v.scalar_tensor_tensor(
                                out=x1b[:, (t % 2) * D + h2 * 512:(t % 2) * D + (h2 + 1) * 512],
                                in0=pp[h2][:, :], scalar=rs,
                                in1=cbh[:, h2 * 512:(h2 + 1) * 512],
                                op0=ALU.mult, op1=ALU.add)
                        v.drain()
                        a = v.tensor_add(x1b[:, (t % 2) * D:(t % 2 + 1) * D],
                                         x1b[:, (t % 2) * D:(t % 2 + 1) * D], x)
                        v.drain()
                        sv.inc(a)
                    x1_done[t] = sv.v

                    @blk.sync
                    def _(sy, t=t):
                        sv.w(sy)
                        so.inc(sy.dma_start(out=dst_ap[t],
                                            in_=x1b[:, (t % 2) * D:(t % 2 + 1) * D]), 16)
                        so.w(sy)
                        sk.inc(sy.nop())
                    x1_store_mark[t] = sk.v

                tA_map = {}

                def do_transA(t):
                    if t >= n_tiles:
                        return
                    tA_map[t] = emit_transA(t)

                emit_load(0)
                emit_stats(0)
                do_transA(0)
                emit_evacT(0)
                emit_load(1)
                emit_stats(1)
                emit_fc(0)
                emit_gelu(0)

                for t in range(n_tiles):
                    do_transA(t + 1)
                    emit_evacT(t + 1)
                    emit_fc(t + 1)
                    emit_gb(t)
                    emit_gelu(t + 1)
                    if t >= 1:
                        emit_proj(t - 1)
                        emit_x1(t - 1)
                    s01 = emit_transB_half(t, 0)
                    emit_bevacs_half(t, 0, s01)
                    s23 = emit_transB_half(t, 1)
                    emit_bevacs_half(t, 1, s23)
                    transB_done[t] = transB_marks[t][3]
                    emit_load(t + 2)
                    emit_stats(t + 2)

                emit_proj(n_tiles - 1)
                emit_x1(n_tiles - 1)

                @blk.sync
                def _(sy):
                    so.w(sy)

        def qkv_phase(m32bf):
            with ExitStack() as ph:
                wqT = ph.enter_context(nc.sbuf_tensor("a2_wqT", [P, DC * D], BF16))
                wkT = ph.enter_context(nc.sbuf_tensor("a2_wkT", [P, DC * D], BF16))
                wvT = ph.enter_context(nc.sbuf_tensor("a2_wvT", [P, DC * D], BF16))
                xb = ph.enter_context(nc.sbuf_tensor("a2_xb", [P, 3 * D], F32))
                xnb = ph.enter_context(nc.sbuf_tensor("a2_xnb", [P, 2 * D], BF16))
                xnT = ph.enter_context(nc.sbuf_tensor("a2_xnT", [P, 2 * D], BF16))
                qb = ph.enter_context(nc.sbuf_tensor("a2_qb", [P, 2 * D], BF16))
                kb = ph.enter_context(nc.sbuf_tensor("a2_kb", [P, 2 * D], BF16))
                vbv = ph.enter_context(nc.sbuf_tensor("a2_vb", [P, 2 * D], BF16))
                qp = ph.enter_context(nc.sbuf_tensor("a2_qp", [P, 2 * D], BF16))
                kp = ph.enter_context(nc.sbuf_tensor("a2_kp", [P, 2 * D], BF16))
                qpT = ph.enter_context(nc.sbuf_tensor("a2_qpT", [P, 2 * D], BF16))
                kpT = ph.enter_context(nc.sbuf_tensor("a2_kpT", [P, 2 * D], BF16))
                rtq = ph.enter_context(nc.sbuf_tensor("a2_rtq", [P, 1024], F32))
                rtk = ph.enter_context(nc.sbuf_tensor("a2_rtk", [P, 1024], F32))
                dmp = ph.enter_context(nc.sbuf_tensor("a2_dmp", [P, D], BF16))
                cse = ph.enter_context(nc.sbuf_tensor("a2_cse", [P, 1024], F32))
                cso = ph.enter_context(nc.sbuf_tensor("a2_cso", [P, 1024], F32))
                tmr = [ph.enter_context(nc.sbuf_tensor(f"a2_tm{i}", [P, 512], F32)) for i in range(4)]
                gtm = [ph.enter_context(nc.sbuf_tensor(f"a2_gtm{i}", [P, 512], F32)) for i in range(4)]
                row32 = ph.enter_context(nc.sbuf_tensor("row32", [1, NT * P], F32))
                psT = ph.enter_context(nc.psum_tensor("a2_psT", [P, 1024], BF16))
                psTqk = ph.enter_context(nc.psum_tensor("a2_psTqk", [P, 1024], BF16))
                psQ = [ph.enter_context(nc.psum_tensor(f"a2_psQ{i}", [P, 512], F32)) for i in range(2)]
                psK = [ph.enter_context(nc.psum_tensor(f"a2_psK{i}", [P, 512], F32)) for i in range(2)]
                psV = [ph.enter_context(nc.psum_tensor(f"a2_psV{i}", [P, 512], F32)) for i in range(2)]
                blk = ph.enter_context(nc.Block())

                @blk.sync
                def _(sy):
                    sd.w(sy)
                    sd.inc(sy.dma_start(out=wqT[:, :], in_=wqT_h[:, 0:DC * D]), 16)
                    sd.inc(sy.dma_start(out=wkT[:, :], in_=wkT_h[:, 0:DC * D]), 16)
                    sd.inc(sy.dma_start(out=wvT[:, :], in_=wvT_h[:, 0:DC * D]), 16)
                    sd.w(sy)
                    sk.inc(sy.nop())

                wt_ready = sk.v
                load_mark = {}
                xnb_mark = {}
                transA_mark = {}
                xnT_mark = {}
                qkv_mark = {}
                qkbv_mark = {}
                qpT_mark = {}
                kpT_mark = {}
                ropeq_mark = {}
                ropek_mark = {}
                q2_mark = {}
                k2_mark = {}
                cs_mark = {}
                qT_trans_mark = {}
                kT_trans_mark = {}
                store_mark = {}

                @blk.gpsimd
                def _(gp):
                    sc.inc(gp.dma_start(out=cse[:, :], in_=cs_h[0]), 16)
                    sc.inc(gp.dma_start(out=cso[:, :], in_=cs_h[1]), 16)
                    sc.w(gp)
                    sq.inc(gp.nop())
                cs_mark[0] = (sq, sq.v)
                cs_mark[1] = (sq, sq.v)

                def cs_of(t):
                    return cse if t % 2 == 0 else cso

                def emit_load(t):
                    if t >= NT:
                        return

                    @blk.sync
                    def _(sy, t=t):
                        if t >= 3:
                            sv.w(sy, q2_mark[t - 3])
                            so.w(sy)
                        sd.inc(sy.dma_start(out=xb[:, (t % 3) * D:(t % 3 + 1) * D], in_=x1_h[t]), 16)
                        sd.w(sy)
                        sk.inc(sy.nop())
                    load_mark[t] = sk.v

                def emit_stats(t):
                    if t >= NT:
                        return

                    @blk.vector
                    def _(v, t=t):
                        v.wait_ge(sk.h, load_mark[t])
                        if t >= 2:
                            v.wait_ge(sp.h, transA_mark[t - 2])
                        x = xb[:, (t % 3) * D:(t % 3 + 1) * D]
                        v.reduce_sum(C(0), x, axis=AX)
                        v.scalar_tensor_tensor(out=dmp[:, 0:D], in0=x, scalar=1.0, in1=x,
                                               op0=ALU.mult, op1=ALU.mult, accum_out=C(1))
                        v.drain()
                        v.tensor_scalar(out=C(0), in0=C(0), scalar1=1.0 / D, scalar2=None, op0=ALU.mult)
                        v.drain()
                        v.tensor_mul(C(6), C(0), C(0))
                        v.tensor_scalar(out=C(2), in0=C(1), scalar1=1.0 / D, scalar2=None, op0=ALU.mult)
                        v.drain()
                        v.tensor_sub(C(2), C(2), C(6))
                        v.drain()
                        v.tensor_scalar(out=C(2), in0=C(2), scalar1=1.0, scalar2=LN_EPS,
                                        op0=ALU.mult, op1=ALU.add)
                        v.drain()
                        rsqrt_col(v, C(3), C(2), C(6), 32.0, 0.05, C(11))
                        a = v.tensor_scalar(out=xnb[:, (t % 2) * D:(t % 2 + 1) * D], in0=x,
                                            scalar1=C(0), scalar2=C(3),
                                            op0=ALU.subtract, op1=ALU.mult)
                        v.drain()
                        sv.inc(a)
                    xnb_mark[t] = sv.v

                def emit_transA(t):
                    if t >= NT:
                        return

                    @blk.tensor
                    def _(te, t=t):
                        sv.w(te, xnb_mark[t])
                        if t >= 1:
                            te.wait_ge(ss.h, xnT_mark[t - 1])
                        for cg in range(DC):
                            sp.inc(te.transpose(psT[:, cg * P:(cg + 1) * P],
                                                xnb[:, (t % 2) * D + cg * P:(t % 2) * D + (cg + 1) * P],
                                                ident_b[:, :]))
                    transA_mark[t] = sp.v

                def emit_qkv(t):
                    if t >= NT:
                        return

                    @blk.tensor
                    def _(te, t=t):
                        te.wait_ge(ss.h, xnT_mark[t])
                        te.wait_ge(sk.h, wt_ready)
                        if t >= 1:
                            te.wait_ge(sv.h, ropeq_mark[t - 1])
                            te.wait_ge(sg.h, ropek_mark[t - 1])
                            te.wait_ge(ss.h, qkbv_mark[t - 1])
                        for dc in range(DC):
                            lhs = xnT[:, (t % 2) * D + dc * P:(t % 2) * D + (dc + 1) * P]
                            for h2 in range(2):
                                o = dc * D + h2 * 512
                                a = te.matmul(psQ[h2][:, :], lhs, wqT[:, o:o + 512],
                                              start=(dc == 0), stop=(dc == DC - 1))
                                if dc == DC - 1:
                                    sp.inc(a)
                                a = te.matmul(psK[h2][:, :], lhs, wkT[:, o:o + 512],
                                              start=(dc == 0), stop=(dc == DC - 1))
                                if dc == DC - 1:
                                    sp.inc(a)
                                a = te.matmul(psV[h2][:, :], lhs, wvT[:, o:o + 512],
                                              start=(dc == 0), stop=(dc == DC - 1))
                                if dc == DC - 1:
                                    sp.inc(a)
                    qkv_mark[t] = sp.v

                def emit_ropeq(t):
                    @blk.vector
                    def _(v, t=t):
                        v.wait_ge(ss.h, qkbv_mark[t])
                        cs2, csm = cs_mark[t]
                        v.wait_ge(cs2.h, csm)
                        cs = cs_of(t)
                        qe = qb.ap()[:, (t % 2) * D:(t % 2 + 1) * D:2]
                        qo = qb.ap()[:, (t % 2) * D + 1:(t % 2 + 1) * D:2]
                        v.tensor_mul(rtq[:, 0:512], qe, cs[:, 0:512])
                        v.tensor_mul(rtq[:, 512:1024], qo, cs[:, 512:1024])
                        v.drain()
                        sv.inc(v.tensor_sub(qp[:, (t % 2) * D:(t % 2) * D + 512],
                                            rtq[:, 0:512], rtq[:, 512:1024]))
                        v.drain()
                        v.tensor_mul(rtq[:, 0:512], qe, cs[:, 512:1024])
                        v.tensor_mul(rtq[:, 512:1024], qo, cs[:, 0:512])
                        v.drain()
                        a = v.tensor_add(qp[:, (t % 2) * D + 512:(t % 2 + 1) * D],
                                         rtq[:, 0:512], rtq[:, 512:1024])
                        v.drain()
                        sv.inc(a)
                        a = v.scalar_tensor_tensor(out=dmp[:, :],
                                                   in0=qp[:, (t % 2) * D:(t % 2 + 1) * D],
                                                   scalar=1.0,
                                                   in1=qp[:, (t % 2) * D:(t % 2 + 1) * D],
                                                   op0=ALU.mult, op1=ALU.mult,
                                                   accum_out=q2cols[:, t:t + 1])
                        v.drain()
                        sv.inc(a)
                    ropeq_mark[t] = sv.v - 1
                    q2_mark[t] = sv.v

                def emit_ropek(t):
                    @blk.gpsimd
                    def _(gp, t=t):
                        gp.wait_ge(ss.h, qkbv_mark[t])
                        cs2, csm = cs_mark[t]
                        gp.wait_ge(cs2.h, csm)
                        cs = cs_of(t)
                        ke = kb.ap()[:, (t % 2) * D:(t % 2 + 1) * D:2]
                        ko = kb.ap()[:, (t % 2) * D + 1:(t % 2 + 1) * D:2]
                        gp.tensor_tensor(rtk[:, 0:512], ke, cs[:, 0:512], ALU.mult)
                        gp.tensor_tensor(rtk[:, 512:1024], ko, cs[:, 512:1024], ALU.mult)
                        gp.drain()
                        sg.inc(gp.tensor_tensor(kp[:, (t % 2) * D:(t % 2) * D + 512],
                                                rtk[:, 0:512], rtk[:, 512:1024], ALU.subtract))
                        gp.drain()
                        gp.tensor_tensor(rtk[:, 0:512], ke, cs[:, 512:1024], ALU.mult)
                        gp.tensor_tensor(rtk[:, 512:1024], ko, cs[:, 0:512], ALU.mult)
                        gp.drain()
                        a = gp.tensor_tensor(kp[:, (t % 2) * D + 512:(t % 2 + 1) * D],
                                             rtk[:, 0:512], rtk[:, 512:1024], ALU.add)
                        gp.drain()
                        sg.inc(a)
                    ropek_mark[t] = sg.v

                def emit_csload(t):
                    if t < 2 or t > NT:
                        return

                    @blk.gpsimd
                    def _(gp, t=t):
                        if t >= 3:
                            sc.w(gp)
                            sq.inc(gp.nop())
                        if t < NT:
                            sv.w(gp, ropeq_mark[t - 2])
                            sc.inc(gp.dma_start(out=cs_of(t)[:, :], in_=cs_h[t]), 16)
                    if t >= 3:
                        cs_mark[t - 1] = (sq, sq.v)

                def emit_k2(t):
                    @blk.vector
                    def _(v, t=t):
                        v.wait_ge(sg.h, ropek_mark[t])
                        a = v.scalar_tensor_tensor(out=dmp[:, :],
                                                   in0=kp[:, (t % 2) * D:(t % 2 + 1) * D],
                                                   scalar=1.0,
                                                   in1=kp[:, (t % 2) * D:(t % 2 + 1) * D],
                                                   op0=ALU.mult, op1=ALU.mult,
                                                   accum_out=k2cols[:, t:t + 1])
                        v.drain()
                        sv.inc(a)
                    k2_mark[t] = sv.v

                def emit_qT(t):
                    @blk.tensor
                    def _(te, t=t):
                        sv.w(te, ropeq_mark[t])
                        if t >= 1:
                            te.wait_ge(ss.h, kpT_mark[t - 1])
                        for cg in range(DC):
                            sp.inc(te.transpose(psTqk[:, cg * P:(cg + 1) * P],
                                                qp[:, (t % 2) * D + cg * P:(t % 2) * D + (cg + 1) * P],
                                                ident_b[:, :]))
                    qT_trans_mark[t] = sp.v

                def emit_kT(t):
                    qpT_mark[t] = ss.v + 1

                    @blk.tensor
                    def _(te, t=t):
                        te.wait_ge(sg.h, ropek_mark[t])
                        te.wait_ge(ss.h, qpT_mark[t])
                        for cg in range(DC):
                            sp.inc(te.transpose(psTqk[:, cg * P:(cg + 1) * P],
                                                kp[:, (t % 2) * D + cg * P:(t % 2) * D + (cg + 1) * P],
                                                ident_b[:, :]))
                    kT_trans_mark[t] = sp.v

                def emit_scalar(t):
                    @blk.scalar
                    def _(s, t=t):
                        s.wait_ge(sp.h, qT_trans_mark[t])
                        if t >= 2:
                            s.wait_ge(sk.h, store_mark.get(t - 2, 0))
                        ss.inc(s.copy(qpT[:, (t % 2) * D:(t % 2 + 1) * D], psTqk[:, :]))
                        s.wait_ge(sp.h, kT_trans_mark[t])
                        ss.inc(s.copy(kpT[:, (t % 2) * D:(t % 2 + 1) * D], psTqk[:, :]))
                        if t + 2 < NT:
                            s.wait_ge(sp.h, transA_mark[t + 2])
                            s.wait_ge(sp.h, qkv_mark[t])
                            ss.inc(s.copy(xnT[:, (t % 2) * D:(t % 2 + 1) * D], psT[:, :]))
                        if t + 1 < NT:
                            s.wait_ge(sp.h, qkv_mark[t + 1])
                            s.wait_ge(sk.h, store_mark.get(t - 1, 0))
                            if t >= 1:
                                s.wait_ge(sv.h, ropeq_mark[t])
                                s.wait_ge(sg.h, ropek_mark[t])
                            for h2 in range(2):
                                ss.inc(s.copy(qb[:, ((t + 1) % 2) * D + h2 * 512:((t + 1) % 2) * D + (h2 + 1) * 512],
                                              psQ[h2][:, :]))
                                ss.inc(s.copy(kb[:, ((t + 1) % 2) * D + h2 * 512:((t + 1) % 2) * D + (h2 + 1) * 512],
                                              psK[h2][:, :]))
                                ss.inc(s.copy(vbv[:, ((t + 1) % 2) * D + h2 * 512:((t + 1) % 2) * D + (h2 + 1) * 512],
                                              psV[h2][:, :]))
                    assert qpT_mark[t] == ss.v - (6 if t + 1 < NT else 0) - (1 if t + 2 < NT else 0) - 1
                    kpT_mark[t] = qpT_mark[t] + 1
                    if t + 2 < NT:
                        xnT_mark[t + 2] = kpT_mark[t] + 1
                    if t + 1 < NT:
                        qkbv_mark[t + 1] = ss.v

                def emit_store(t):
                    @blk.sync
                    def _(sy, t=t):
                        ss.w(sy, kpT_mark[t])
                        sv.w(sy, q2_mark[t])
                        sg.w(sy, ropek_mark[t])
                        so.inc(sy.dma_start(out=qT_h[t], in_=qpT[:, (t % 2) * D:(t % 2 + 1) * D]), 16)
                        so.inc(sy.dma_start(out=kT_h[t], in_=kpT[:, (t % 2) * D:(t % 2 + 1) * D]), 16)
                        so.inc(sy.dma_start(out=v_h[t], in_=vbv[:, (t % 2) * D:(t % 2 + 1) * D]), 16)
                        so.w(sy)
                        sk.inc(sy.nop())
                    store_mark[t] = sk.v

                emit_load(0)
                emit_stats(0)
                emit_transA(0)

                @blk.scalar
                def _(s):
                    s.wait_ge(sp.h, transA_mark[0])
                    ss.inc(s.copy(xnT[:, 0:D], psT[:, :]))
                xnT_mark[0] = ss.v
                emit_load(1)
                emit_stats(1)
                emit_transA(1)
                emit_qkv(0)

                @blk.scalar
                def _(s):
                    s.wait_ge(sp.h, transA_mark[1])
                    ss.inc(s.copy(xnT[:, D:2 * D], psT[:, :]))
                    s.wait_ge(sp.h, qkv_mark[0])
                    for h2 in range(2):
                        ss.inc(s.copy(qb[:, h2 * 512:(h2 + 1) * 512], psQ[h2][:, :]))
                        ss.inc(s.copy(kb[:, h2 * 512:(h2 + 1) * 512], psK[h2][:, :]))
                        ss.inc(s.copy(vbv[:, h2 * 512:(h2 + 1) * 512], psV[h2][:, :]))
                xnT_mark[1] = ss.v - 6
                qkbv_mark[0] = ss.v
                emit_load(2)
                emit_stats(2)

                for t in range(NT):
                    emit_ropeq(t)
                    emit_ropek(t)
                    emit_csload(t + 2)
                    emit_k2(t)
                    emit_transA(t + 2)
                    emit_qT(t)
                    emit_qkv(t + 1)
                    emit_kT(t)
                    emit_scalar(t)
                    emit_store(t)
                    emit_load(t + 3)
                    emit_stats(t + 3)

                @blk.vector
                def _(v):
                    sv.w(v)
                    a = v.reduce_max(C(9), k2cols[:, 0:NT], axis=AX)
                    v.drain()
                    sv.inc(a)

                @blk.tensor
                def _(te):
                    sv.w(te)
                    ss.w(te)
                    sp.inc(te.transpose(psQ[0][0:1, 0:P], C(9), ident_f[:, :]))
                    sp.inc(te.transpose(psK[0][0:NT, 0:P], q2cols[:, 0:NT], ident_f[:, :]))

                @blk.vector
                def _(v):
                    sp.w(v)
                    a = v.reduce_max(cols[0:1, 10:11], psQ[0][0:1, 0:P], axis=AX)
                    v.drain()
                    sv.inc(a)

                @blk.scalar
                def _(s):
                    sp.w(s)
                    ss.inc(s.copy(rtq[0:NT, 0:P], psK[0][0:NT, 0:P]))

                @blk.sync
                def _(sy):
                    ss.w(sy)
                    sd.inc(sy.dma_start(out=row32[0:1, 0:NT * P], in_=rtq[0:NT, 0:P]), 16)
                    sd.w(sy)
                    sk.inc(sy.nop())

                @blk.vector
                def _(v):
                    sk.w(v)
                    sv.w(v)
                    v.tensor_scalar(out=row32[0:1, 0:NT * P], in0=row32[0:1, 0:NT * P],
                                    scalar1=cols[0:1, 10:11], scalar2=0.5, op0=ALU.add, op1=ALU.mult)
                    v.drain()
                    a = v.tensor_copy(m32bf[0:1, 0:NT * P], row32[0:1, 0:NT * P])
                    v.drain()
                    sv.inc(a)

                @blk.sync
                def _(sy):
                    sv.w(sy)
                    so.inc(sy.dma_start(out=m32_hd[0:1, :], in_=m32bf[0:1, 0:NT * P]), 16)
                    so.w(sy)

        def attn_phase(m32bf):
            with ExitStack() as ph:
                qtg = ph.enter_context(nc.sbuf_tensor("b1_qtg", [P, 2 * D], BF16))
                x1g = ph.enter_context(nc.sbuf_tensor("b1_x1g", [P, 2 * D], F32))
                x2b = ph.enter_context(nc.sbuf_tensor("b1_x2b", [P, 2 * D], F32))
                m32g = ph.enter_context(nc.sbuf_tensor("b1_m32g", [1, (NT // 2) * P], BF16))
                cmask = ph.enter_context(nc.sbuf_tensor("b1_cmask", [P, 4 * 256], BF16))
                pcol = ph.enter_context(nc.sbuf_tensor("b1_pid", [1, 1], mybir.dt.uint32))
                kbuf = [ph.enter_context(nc.sbuf_tensor(f"b1_kbuf{i}", [P, 4 * D], BF16)) for i in range(2)]
                vbuf = [ph.enter_context(nc.sbuf_tensor(f"b1_vbuf{i}", [P, 4 * D], BF16)) for i in range(2)]
                pbuf = [ph.enter_context(nc.sbuf_tensor(f"b1_p{i}", [P, 256], BF16)) for i in range(2)]
                rcol = ph.enter_context(nc.sbuf_tensor("b1_rcol", [P, 2], F32))
                psS = [ph.enter_context(nc.psum_tensor(f"b1_psS{i}", [P, 512], F32)) for i in range(2)]
                psA = [[ph.enter_context(nc.psum_tensor(f"b1_psA{i}{h}", [P, 512], F32))
                        for h in range(2)] for i in range(2)]
                psN = [ph.enter_context(nc.psum_tensor(f"b1_psN{i}", [P, 512], F32)) for i in range(2)]
                rq = ph.enter_context(nc.sync.register("b1_rq"))
                rx = ph.enter_context(nc.sync.register("b1_rx"))
                rm = ph.enter_context(nc.sync.register("b1_rm"))
                blk = ph.enter_context(nc.Block())

                kk = kT_h.rearrange("n p d -> p n d")
                vv = v_h.rearrange("n p d -> p n d")
                qtg4 = qtg.ap().rearrange("p (i c t) -> p c i t", i=2, c=DC, t=P)
                NGB = NT // 4

                TL = P * D
                NGB0 = NT // 4

                @blk.sync
                def _(sy):
                    sd.w(sy)
                    sd.inc(sy.dma_start(out=cmask[:, :], in_=cmask_h[:, :]), 16)
                    sd.inc(sy.dma_start(out=pcol[0:1, 0:1], in_=pid_h[:, :]), 16)
                    sd.w(sy)
                    sy.reg_load(rq, pcol[0:1, 0:1])
                    sy.reg_mul(rq, rq, 2 * TL)
                    sy.reg_load(rx, pcol[0:1, 0:1])
                    sy.reg_mul(rx, rx, 2 * TL)
                    sy.reg_load(rm, pcol[0:1, 0:1])
                    sy.reg_mul(rm, rm, 2 * P)
                    sd.inc(sy.dma_start(
                        out=bass.AP(qTl_h.tensor, 0, [[2 * TL, NGB0], [1, 2 * TL]]),
                        in_=bass.AP(qT_h.tensor, rq, [[4 * TL, NGB0], [1, 2 * TL]]),
                        bounds_check=None), 16)
                    sd.inc(sy.dma_start(
                        out=bass.AP(x1l_h.tensor, 0, [[2 * TL, NGB0], [1, 2 * TL]]),
                        in_=bass.AP(x1_h.tensor, rx,
                                    [[4 * TL, NGB0], [1, 2 * TL]]),
                        bounds_check=None), 16)
                    sd.inc(sy.dma_start(
                        out=bass.AP(m32l_h.tensor, 0, [[2 * P, NGB0], [1, 2 * P]]),
                        in_=bass.AP(m32_hd.tensor, rm, [[4 * P, NGB0], [1, 2 * P]]),
                        bounds_check=None), 16)
                    sd.w(sy)
                    sd.inc(sy.dma_start(out=m32g[0:1, :], in_=m32l_h[0:1, 0:(NT // 2) * P]), 16)
                    sd.w(sy)
                    sk.inc(sy.nop())

                ss0 = ss.v
                sp0 = sp.v
                sg0 = sg.v
                sk0 = sk.v
                exp_mark = []
                pcons_mark = []
                sgm_mark = []
                batch_mark = []
                kload_mark = []
                gload_mark = []
                s_mark = []
                sval = sp0
                ssv = ss0
                sgv = sg.v
                skv = sk0
                pcons_d = {}
                batch_d = {}
                e = 0
                bgc = 0
                for gq in range(NGB):
                    jmax = 4 * gq + 3
                    skv += 1
                    gload_mark.append(skv)
                    for j in range(jmax + 1):
                        if j % 4 == 0:
                            skv += 1
                            kload_mark.append(skv)
                            bgc += 1
                        sval += 1
                        s_mark.append(sval)
                        ssv += 1
                        exp_mark.append(ssv)
                        if j >= 4 * gq:
                            sgv += 1
                        sgm_mark.append(sgv)
                        if j > 0:
                            sval += 1
                            pcons_d[e - 1] = sval
                            if (j - 1) % 4 == 3:
                                batch_d[bgc - 1 - (1 if j % 4 == 0 else 0)] = sval
                        e += 1
                    sval += 6
                    pcons_d[e - 1] = sval
                    batch_d[bgc - 1] = sval
                for i in range(len(exp_mark)):
                    pcons_mark.append(pcons_d[i])
                for i in range(bgc):
                    batch_mark.append(batch_d[i])
                e_of = []
                bg_of = []
                eacc = 0
                bacc = 0
                for gq in range(NGB):
                    e_of.append(eacc)
                    bg_of.append(bacc)
                    eacc += 4 * gq + 4
                    bacc += gq + 1

                for gq in range(NGB):
                    jmax = 4 * gq + 3
                    e0 = e_of[gq]
                    b0 = bg_of[gq]
                    nb = gq + 1

                    @blk.sync
                    def _(sy, gq=gq, jmax=jmax, b0=b0, nb=nb):
                        if gq >= 2:
                            so.w(sy)
                            sv.w(sy)
                        sd.inc(sy.dma_start(out=qtg[:, 0:D], in_=qTl_h[2 * gq]), 16)
                        sd.inc(sy.dma_start(out=qtg[:, D:2 * D], in_=qTl_h[2 * gq + 1]), 16)
                        sd.inc(sy.dma_start(out=x1g[:, 0:D], in_=x1l_h[2 * gq]), 16)
                        sd.inc(sy.dma_start(out=x1g[:, D:2 * D], in_=x1l_h[2 * gq + 1]), 16)
                        sd.w(sy)
                        sk.inc(sy.nop())
                        for bi in range(nb):
                            bgg = b0 + bi
                            if bgg >= 2:
                                sy.wait_ge(sp.h, batch_mark[bgg - 2])
                            j = bi * 4
                            n = min(4, jmax + 1 - j)
                            sd.inc(sy.dma_start(out=kbuf[bgg % 2][:, 0:n * D],
                                                in_=kk[:, j:j + n, :]), 16)
                            sd.inc(sy.dma_start(out=vbuf[bgg % 2][:, 0:n * D],
                                                in_=vv[:, j:j + n, :]), 16)
                            sd.w(sy)
                            sk.inc(sy.nop())

                    @blk.tensor
                    def _(te, gq=gq, jmax=jmax, e0=e0, b0=b0):
                        te.wait_ge(sk.h, gload_mark[gq])

                        def av(j, final):
                            eg = e0 + j
                            te.wait_ge(ss.h, exp_mark[eg])
                            if j >= 4 * gq:
                                te.wait_ge(sg.h, sgm_mark[eg])
                            vb = vbuf[(b0 + j // 4) % 2]
                            for sub in range(2):
                                lhs = pbuf[eg % 2][:, sub * P:(sub + 1) * P]
                                for h2 in range(2):
                                    a = te.matmul(psA[sub][h2][:, :], lhs,
                                                  vb[:, (j % 4) * D + h2 * 512:(j % 4) * D + (h2 + 1) * 512],
                                                  start=(j == 0), stop=final)
                                    if final:
                                        sp.inc(a)
                                a = te.matmul(psN[sub][:, 0:1], lhs, onescol_b[:, 0:1],
                                              start=(j == 0), stop=final)
                                if final:
                                    sp.inc(a)
                                elif sub == 1:
                                    sp.inc(a)
                        for j in range(jmax + 1):
                            eg = e0 + j
                            bgg = b0 + j // 4
                            te.wait_ge(sk.h, kload_mark[bgg])
                            if eg >= 2:
                                te.wait_ge(ss.h, exp_mark[eg - 2])
                            kv = kbuf[bgg % 2]
                            for dc in range(DC):
                                te.matmul(psS[eg % 2][:, 0:256],
                                          kv[:, (j % 4) * D + dc * P:(j % 4) * D + dc * P + P],
                                          qtg4[:, dc], start=(dc == 0), stop=False)
                            a = te.matmul(psS[eg % 2][:, 0:256], negones_b[0:1, :],
                                          m32g[0:1, gq * 256:(gq + 1) * 256], start=False, stop=True)
                            sp.inc(a)
                            assert sp.v == s_mark[eg]
                            if j > 0:
                                av(j - 1, False)
                        av(jmax, True)
                        assert sp.v == pcons_mark[e0 + jmax], (sp.v, pcons_mark[e0 + jmax])

                    @blk.scalar
                    def _(s, gq=gq, jmax=jmax, e0=e0):
                        for j in range(jmax + 1):
                            eg = e0 + j
                            if eg >= 2:
                                s.wait_ge(sp.h, pcons_mark[eg - 2])
                            s.wait_ge(sp.h, s_mark[eg])
                            a = s.activation(out=pbuf[eg % 2][:, :], in_=psS[eg % 2][:, 0:256],
                                             func=AF.Exp, bias=0.0, scale=1.0 / 32)
                            ss.inc(a)
                        assert ss.v == exp_mark[e0 + jmax]

                    @blk.gpsimd
                    def _(gp, gq=gq, jmax=jmax, e0=e0):
                        for j in range(4 * gq, jmax + 1):
                            eg = e0 + j
                            m = j - 4 * gq
                            gp.wait_ge(ss.h, exp_mark[eg])
                            a = gp.tensor_tensor(pbuf[eg % 2][:, :], pbuf[eg % 2][:, :],
                                                 cmask[:, m * 256:(m + 1) * 256], ALU.mult)
                            gp.drain()
                            sg.inc(a)
                        assert sg.v == sgm_mark[e0 + jmax]

                    @blk.vector
                    def _(v, gq=gq, jmax=jmax, e0=e0):
                        v.wait_ge(sp.h, pcons_mark[e0 + jmax])
                        so.w(v)
                        for sub in range(2):
                            v.tensor_scalar(out=rcol[:, sub:sub + 1], in0=psN[sub][:, 0:1],
                                            scalar1=SC21, scalar2=5e-13, op0=ALU.mult, op1=ALU.max)
                        v.drain()
                        v.reciprocal(rcol[:, 0:1], rcol[:, 0:1])
                        v.reciprocal(rcol[:, 1:2], rcol[:, 1:2])
                        v.drain()
                        v.tensor_scalar(out=rcol[:, 0:2], in0=rcol[:, 0:2], scalar1=SC21,
                                        scalar2=None, op0=ALU.mult)
                        v.drain()
                        a = None
                        for sub in range(2):
                            for h2 in range(2):
                                a = v.scalar_tensor_tensor(
                                    out=x2b[:, sub * D + h2 * 512:sub * D + (h2 + 1) * 512],
                                    in0=psA[sub][h2][:, :], scalar=rcol[:, sub:sub + 1],
                                    in1=x1g[:, sub * D + h2 * 512:sub * D + (h2 + 1) * 512],
                                    op0=ALU.mult, op1=ALU.add)
                        v.drain()
                        sv.inc(a)

                    @blk.sync
                    def _(sy, gq=gq):
                        sv.w(sy)
                        so.inc(sy.dma_start(out=x2_h[2 * gq], in_=x2b[:, 0:D]), 16)
                        so.inc(sy.dma_start(out=x2_h[2 * gq + 1], in_=x2b[:, D:2 * D]), 16)

                @blk.sync
                def _(sy):
                    so.w(sy)

        ffn_phase(x0_h, x1_h, NT, "a1")
        with ExitStack() as mid:
            m32bf = mid.enter_context(nc.sbuf_tensor("m32bf", [1, NT * P], BF16))
            qkv_phase(m32bf)
            attn_phase(m32bf)
        ffn_phase(x2_h, out_h, NT // 2, "b2")

    return nc


def kernel(L0_attn_x, L0_attn_Wq, L0_attn_Wk, L0_attn_Wv,
           L0_ffn_W_fc, L0_ffn_ln_w, L0_ffn_ln_b, L0_ffn_W_proj):
    x = np.ascontiguousarray(np.asarray(L0_attn_x, dtype=np.float32))
    B, S, d = x.shape
    NT = S // P
    nc = build_program(NT=NT)
    wq = np.ascontiguousarray(np.asarray(L0_attn_Wq, np.float32))
    wk = np.ascontiguousarray(np.asarray(L0_attn_Wk, np.float32))
    wv = np.ascontiguousarray(np.asarray(L0_attn_Wv, np.float32))
    wfc = np.ascontiguousarray(np.asarray(L0_ffn_W_fc, np.float32))
    lnw = np.ascontiguousarray(np.asarray(L0_ffn_ln_w, np.float32).reshape(HC, P))
    lnb = np.ascontiguousarray(np.asarray(L0_ffn_ln_b, np.float32).reshape(HC, P))
    wpj = np.ascontiguousarray(np.asarray(L0_ffn_W_proj, np.float32))
    import ml_dtypes
    import ctypes
    _libm = ctypes.CDLL("libm.so.6")
    _libm.sincosf.restype = None
    _libm.sincosf.argtypes = [ctypes.c_float, ctypes.POINTER(ctypes.c_float),
                              ctypes.POINTER(ctypes.c_float)]
    inv32 = (10000.0 ** (-np.arange(0, d, 2, dtype=np.float64) / d)).astype(np.float32)
    ang32 = (np.arange(S, dtype=np.float64)[:, None] * inv32.astype(np.float64)).astype(np.float32)
    ctab = np.empty(S * 512, np.float32)
    stab = np.empty(S * 512, np.float32)
    _s = ctypes.c_float()
    _c = ctypes.c_float()
    _bs, _bc = ctypes.byref(_s), ctypes.byref(_c)
    _fn = _libm.sincosf
    for _i, _x in enumerate(ang32.reshape(-1).tolist()):
        _fn(_x, _bs, _bc)
        stab[_i] = _s.value
        ctab[_i] = _c.value
    cstab = np.concatenate([ctab.reshape(NT, P, 512), stab.reshape(NT, P, 512)],
                           axis=2)
    kkq = np.arange(256)[None, :]
    kkk = np.arange(P)[:, None]
    cmasks = []
    for pi in range(2):
        blocks = [(kkq - kkk + (2 * pi - m) * P >= 0).astype(np.float32) for m in range(4)]
        cmasks.append(np.concatenate(blocks, axis=1).astype(ml_dtypes.bfloat16))
    in_maps = []
    for c in range(8):
        b = c // 2
        in_maps.append({
            "xseq": np.ascontiguousarray(x[b].reshape(NT, P, d)),
            "Wq": wq, "Wk": wk, "Wv": wv, "Wfc": wfc,
            "lnw": lnw, "lnb": lnb, "Wproj": wpj,
            "pid": np.array([[c % 2]], np.uint32),
            "cmask": cmasks[c % 2],
            "cstab": cstab,
        })
    res = run_bass_kernel_spmd(nc, in_maps, core_ids=list(range(8)))
    out = np.empty((B, S, d), np.asarray(res.results[0]["out"]).dtype)
    for b in range(B):
        for c2, pi in ((2 * b, 0), (2 * b + 1, 1)):
            o = np.asarray(res.results[c2]["out"]).reshape(NT // 2, P, d)
            for i in range(NT // 2):
                gtile = 2 * (i - i % 2) + 2 * pi + i % 2
                out[b, gtile * P:(gtile + 1) * P] = o[i]
    return out
